# version 21
# speedup vs baseline: 1.0221x; 1.0028x over previous
import numpy as np
from contextlib import ExitStack
import concourse.bass as bass
import concourse.mybir as mybir

F32 = mybir.dt.float32
BF16 = mybir.dt.bfloat16
OP = mybir.AluOpType
AF = mybir.ActivationFunctionType
AX = mybir.AxisListType

ENGINES = ("tensor", "vector", "scalar", "gpsimd", "sync")

N_TOK = 2048
D = 1024
L1A_MAXCH = 16
L1A_SUB = 6
L0A_NOPIPE = False
TCH = 16
DCH = 8


class Sched:

    def __init__(self, nc, es):
        self.nc = nc
        self.sems = {e: es.enter_context(nc.semaphore(f"s_{e}")) for e in ENGINES}
        self.cnt = {e: 0 for e in ENGINES}
        self.q = {e: [] for e in ENGINES}
        self.last_wait = {e: {o: 0 for o in ENGINES} for e in ENGINES}
        self.extra_sems = {}

    def sem(self, name, es):
        if name not in self.extra_sems:
            self.extra_sems[name] = es.enter_context(self.nc.semaphore(name))
        return self.extra_sems[name]

    def emit(self, e, fn, waits=(), inc=0, dma=False, self_waits=(), sem_waits=(),
             custom_sem=None):
        if inc == 0 and dma:
            inc = 16
        real_waits = []
        for (we, wc) in self_waits:
            assert we == e
            if wc > self.last_wait[e][we]:
                self.last_wait[e][we] = wc
                real_waits.append((we, wc))
        for (we, wc) in waits:
            if we == e:
                continue
            if wc > self.last_wait[e][we]:
                self.last_wait[e][we] = wc
                real_waits.append((we, wc))
        inc_val = inc
        sems = self.sems

        def thunk(eng, fn=fn, real_waits=real_waits, inc_val=inc_val, e=e,
                  sem_waits=tuple(sem_waits), custom_sem=custom_sem):
            for (sh, sv) in sem_waits:
                eng.wait_ge(sh, sv)
            for (we, wc) in real_waits:
                eng.wait_ge(sems[we], wc)
            inst = fn(eng)
            if custom_sem is not None and inst is not None:
                inst.then_inc(custom_sem)
            elif inc_val and inst is not None:
                inst.then_inc(sems[e], inc_val)
        self.q[e].append(thunk)
        if inc:
            self.cnt[e] += inc
        return (e, self.cnt[e])

    def all_milestones(self):
        return [(e, self.cnt[e]) for e in ENGINES if self.cnt[e] > 0]

    def barrier(self):
        ms = self.all_milestones()
        for e in ENGINES:
            def mk(eng, ms=ms, e=e):
                return None
            self.emit(e, mk, waits=ms, inc=0)

    def finish(self):
        ms = self.all_milestones()
        for e in ENGINES:
            self.emit(e, lambda eng: None, waits=ms, inc=0)

    def build(self):
        nc = self.nc
        with nc.Block() as block:
            for e in ENGINES:
                def runner(eng, e=e):
                    for t in self.q[e]:
                        t(eng)
                getattr(block, e)(runner)



def host_prep(ins):
    import ml_dtypes
    bf = ml_dtypes.bfloat16
    out = {}

    def b(x):
        return np.ascontiguousarray(x.astype(bf))

    def f(x):
        return np.ascontiguousarray(x.astype(np.float32))

    NH, DQK, DV, VDIM = 8, 128, 256, 2048
    perm = np.concatenate([np.arange(0, DQK, 2), np.arange(1, DQK, 2)])
    Wq = ins["L0_attn_Wq"].reshape(D, NH, DQK)[:, :, perm].reshape(D, D)
    Wk = ins["L0_attn_Wk"].reshape(D, NH, DQK)[:, :, perm].reshape(D, D)
    out["wq0"] = b(Wq)
    out["wk0"] = b(Wk)
    out["wv0"] = b(ins["L0_attn_Wv"])
    out["wg0"] = b(ins["L0_attn_Wg"])
    out["wo0"] = b(ins["L0_attn_Wo"])
    out["gnw0"] = f(ins["L0_attn_g_norm_w"]).reshape(1, DV)
    t = np.arange(N_TOK, dtype=np.float64)
    j = np.arange(0, DQK, 2, dtype=np.float64)
    inv_freq = 1.0 / (10000.0 ** (j / DQK))
    ang = t[None, :] * inv_freq[:, None]
    cos, sin = np.cos(ang), np.sin(ang)
    s_h = np.log2(1.0 - 2.0 ** (-5.0 - np.arange(NH, dtype=np.float64)))
    qscale = DQK ** (-0.5)
    cq = np.zeros((NH, 2, 128, N_TOK), np.float32)
    ck = np.zeros((NH, 2, 128, N_TOK), np.float32)
    for h in range(NH):
        a = np.exp2(s_h[h] * t) * qscale
        bk = np.exp2(-s_h[h] * t)
        for tabs, sc in ((cq, a), (ck, bk)):
            tabs[h, 0, 0:64] = (cos * sc[None, :]).astype(np.float32)
            tabs[h, 0, 64:128] = (cos * sc[None, :]).astype(np.float32)
            tabs[h, 1, 0:64] = (-sin * sc[None, :]).astype(np.float32)
            tabs[h, 1, 64:128] = (sin * sc[None, :]).astype(np.float32)
    out["ropeq"] = b(cq.reshape(NH * 256, N_TOK))
    out["ropek"] = b(ck.reshape(NH * 256, N_TOK))

    out["phiwT"] = b(ins["L0_ffn_phi_w"].T)
    out["phib"] = b(ins["L0_ffn_phi_b"].reshape(1, 2048))
    out["qw1T"] = b(ins["L1_attn_q_w"].T)
    out["kw1T"] = b(ins["L1_attn_k_w"].T)
    out["vw1T"] = b(ins["L1_attn_v_w"].T)
    out["ow1T"] = b(ins["L1_attn_o_w"].T)
    out["csw1T"] = b(ins["L1_attn_cs_w1"].T)
    out["csb1"] = f(ins["L1_attn_cs_b1"]).reshape(2, 128).T.copy()
    out["csw2T"] = b(ins["L1_attn_cs_w2"].T)
    out["csb2"] = f(ins["L1_attn_cs_b2"]).reshape(1, 16)
    out["dparams"] = f(ins["L1_attn_decay_params"]).reshape(1, 16 * 64)

    E8, F_, TF = 8, 1024, 2048
    out["gatewT"] = b(ins["L1_ffn_gate_w"].T)
    out["gateb"] = f(ins["L1_ffn_gate_b"]).reshape(1, 8)
    permF = np.concatenate([np.arange(0, TF, 2), np.arange(1, TF, 2)])
    w1p = ins["L1_ffn_w1"][:, permF, :]
    w3p = ins["L1_ffn_w3"][:, permF, :]
    out["w13T"] = b(np.stack([np.transpose(w1p, (0, 2, 1)),
                              np.transpose(w3p, (0, 2, 1))]))
    out["b1p"] = f(ins["L1_ffn_b1"][:, permF]).reshape(E8, TF // 128, 128).transpose(0, 2, 1).copy()
    out["b3p"] = f(ins["L1_ffn_b3"][:, permF]).reshape(E8, TF // 128, 128).transpose(0, 2, 1).copy()
    out["w2T"] = b(np.transpose(ins["L1_ffn_w2"], (0, 2, 1)))
    out["b2"] = f(ins["L1_ffn_b2"])
    sw1p = ins["L1_ffn_sw1"][permF, :]
    sw3p = ins["L1_ffn_sw3"][permF, :]
    out["sw13T"] = b(np.stack([sw1p.T, sw3p.T]))
    out["sb1p"] = f(ins["L1_ffn_sb1"][permF]).reshape(TF // 128, 128).T.copy()
    out["sb3p"] = f(ins["L1_ffn_sb3"][permF]).reshape(TF // 128, 128).T.copy()
    out["sw2T"] = b(ins["L1_ffn_sw2"].T)
    out["sb2"] = f(ins["L1_ffn_sb2"]).reshape(1, 1024)

    r_ = np.arange(128)
    maskst = np.zeros((128, 4 * 512), np.float32)
    for r in range(4):
        x_, y_ = np.meshgrid(r_, np.arange(512), indexing="ij")
        maskst[:, r * 512:(r + 1) * 512] = (x_ <= y_ - 128 * r)
    out["maskst"] = b(maskst)
    rr, tt = np.meshgrid(r_, r_, indexing="ij")
    same = (rr // 32) == (tt // 32)
    out["lotri32"] = f((rr <= tt) * same)
    out["ustri32"] = f((rr > tt) * same)
    out["maskbd32"] = b((rr <= tt) * same)
    ee, pp = np.meshgrid(r_, np.arange(32), indexing="ij")
    out["sel128"] = b((ee % 32) == pp)
    out["rep32"] = f(((ee % 32) == pp).T)
    return out



def host_prep_percore(ins, core):
    out = {}
    out["x_in"] = np.ascontiguousarray(ins["L0_attn_hidden_states"][core].astype(np.float32))
    idx = list(range(4 * core, 4 * core + 4)) + list(range(32 + 4 * core, 32 + 4 * core + 4))
    import ml_dtypes
    out["expw"] = np.ascontiguousarray(
        ins["L0_ffn_expert_w"][idx].astype(np.float32).astype(ml_dtypes.bfloat16))
    out["expb"] = np.ascontiguousarray(
        ins["L0_ffn_expert_b"][idx].astype(np.float32).reshape(1, 8 * 1024))
    return out



def alloc_consts(nc, es, S):
    c = {}
    c["identb"] = es.enter_context(nc.sbuf_tensor("identb", [128, 128], BF16))
    c["ones1f"] = es.enter_context(nc.sbuf_tensor("ones1f", [1, 128], F32))
    c["ones1b"] = es.enter_context(nc.sbuf_tensor("ones1b", [1, 128], BF16))
    c["ones32f"] = es.enter_context(nc.sbuf_tensor("ones32f", [1, 32], F32))
    c["maskSt"] = es.enter_context(nc.sbuf_tensor("maskSt", [128, 4 * 512], BF16))
    c["lotri_blk32"] = es.enter_context(nc.sbuf_tensor("lotri_blk32", [128, 128], F32))
    c["ustri_blk32"] = es.enter_context(nc.sbuf_tensor("ustri_blk32", [128, 128], F32))
    c["mask_bd32"] = es.enter_context(nc.sbuf_tensor("mask_bd32", [128, 128], BF16))
    c["sel128"] = es.enter_context(nc.sbuf_tensor("c_sel128", [128, 32], BF16))
    c["rep32"] = es.enter_context(nc.sbuf_tensor("c_rep32", [32, 128], F32))
    c["epsb"] = es.enter_context(nc.sbuf_tensor("epsb", [128, 1], F32))
    return c


def emit_consts(nc, S, c, Wc):
    def g(fn):
        return S.emit("gpsimd", fn, inc=1)

    g(lambda e: e.memset(c["identb"][:], 1.0))
    g(lambda e: e.affine_select(out=c["identb"][:], in_=c["identb"][:],
                                compare_op=OP.is_equal, fill=0.0, base=0,
                                pattern=[[-1, 128]], channel_multiplier=1))
    g(lambda e: e.memset(c["ones1f"][:], 1.0))
    g(lambda e: e.memset(c["ones1b"][:], 1.0))
    g(lambda e: e.memset(c["ones32f"][:], 1.0))
    mg = g(lambda e: e.memset(c["epsb"][:], 1e-5))
    md = None
    for nm, key in (("maskst", "maskSt"), ("lotri32", "lotri_blk32"),
                    ("ustri32", "ustri_blk32"), ("maskbd32", "mask_bd32"),
                    ("sel128", "sel128"), ("rep32", "rep32")):
        md = S.emit("sync", lambda e, nm=nm, key=key: e.dma_start(
            out=c[key][:], in_=Wc[nm][:, :]), dma=True)
    return [mg, md]


def emit_ln_chunk(nc, S, st, i, x_src_ms, xch, stats, xsq, xlns_tile, after=()):
    sl = lambda name: stats[name][:, i:i + 1]
    m1 = S.emit("scalar", lambda e: e.activation(xsq[:], xch[:], AF.Square,
                                                 accum_out=sl("sumsq")),
                waits=[x_src_ms] + list(after), inc=1)
    m2 = S.emit("vector", lambda e: e.reduce_sum(sl("s").unsqueeze(2), xch[:].rearrange("p (a b) -> p a b", a=1), axis=AX.X),
                waits=[x_src_ms] + list(after), inc=1)
    S.emit("vector", lambda e: e.tensor_scalar_mul(sl("m"), sl("s"), 1.0 / 1024.0), inc=0)
    S.emit("vector", lambda e: e.tensor_scalar_mul(sl("v1"), sl("sumsq"), 1.0 / 1024.0),
           waits=[m1], inc=0)
    m3 = S.emit("vector", lambda e: e.scalar_tensor_tensor(sl("mvar"), sl("m"), sl("m"), sl("v1"),
                                                           OP.mult, OP.subtract), inc=1)
    S.emit("scalar", lambda e: e.activation(sl("lnv"), sl("mvar"), AF.Ln, bias=1e-5, scale=-1.0),
           waits=[m3], inc=0)
    m4 = S.emit("scalar", lambda e: e.activation(sl("rstd"), sl("lnv"), AF.Exp, scale=-0.5), inc=1)
    S.emit("vector", lambda e: e.tensor_scalar(sl("nm"), sl("m"), sl("rstd"), -1.0, OP.mult, OP.mult),
           waits=[m4], inc=0)
    m5 = S.emit("vector", lambda e: e.tensor_scalar(xlns_tile, xch[:], sl("rstd"), sl("nm"),
                                                    OP.mult, OP.add), inc=1)
    return m5


def alloc_stats(nc, es, prefix):
    names = ["s", "sumsq", "m", "v1", "mvar", "lnv", "rstd", "nm"]
    return {n: es.enter_context(nc.sbuf_tensor(f"{prefix}_{n}", [128, 16], F32)) for n in names}



def emit_l0_attn(nc, S, es, c, psum, x_ap, xout_ap, W, consts_ms, n_heads=8):
    NH = n_heads
    sb = lambda n, s, d: es.enter_context(nc.sbuf_tensor(n, s, d))

    gnb = sb("gnb", [128, 256], F32)
    xch = [sb(f"l0a_xch{i}", [128, 1024], F32) for i in range(2)]
    stats = alloc_stats(nc, es, "l0a")
    xlns = [sb("l0a_xlns0", [128, 1024], BF16)] * 2
    xln_fm = sb("l0a_xlnfm", [128, DCH * 2048], BF16)

    wq_s = [sb(f"l0a_wq{i}", [128, 8 * 128], BF16) for i in range(2)]
    wk_s = [sb(f"l0a_wk{i}", [128, 8 * 128], BF16) for i in range(2)]
    wv_s = [sb("l0a_wv0", [128, 8 * 256], BF16)] * 2
    wg_s = [sb("l0a_wg0", [128, 8 * 256], BF16)] * 2
    wo_s = [sb("l0a_wo0", [128, 2 * 1024], BF16)] * 2
    rq_t = [sb("l0a_rq0", [128, 2 * 2048], BF16)] * 2
    rk_t = [sb("l0a_rk0", [128, 2 * 2048], BF16)] * 2

    raw = sb("l0a_raw", [128, 2048], F32)
    swb = sb("l0a_swb", [128, 2048], F32)
    p1b = sb("l0a_p1", [128, 2048], F32)
    xsq = p1b[:, 0:1024]
    qh = sb("l0a_qh", [128, 2048], BF16)
    kh = sb("l0a_kh", [128, 2048], BF16)
    vh = sb("l0a_vh", [128, 16 * 256], BF16)
    St = sb("l0a_St", [128, 16 * 512], BF16)
    osq = sb("l0a_osq", [128, 256], F32)
    o_sb = [sb(f"l0a_osb{i}", [128, 16 * 256], BF16) for i in range(2)]
    gsil = [sb("l0a_gsil0", [128, 16 * 256], BF16)] * 2
    ofm = sb("l0a_ofm", [128, 16 * 256], BF16)
    ost = sb("l0a_ost", [128, 48], F32)
    delta = sb("l0a_delta", [128, 16 * 1024], BF16)

    xob = [p1b[:, 0:1024], p1b[:, 1024:2048]]
    pt6 = psum[6].bitcast(BF16)
    gn_row = None

    x_t = x_ap.rearrange("(i p) d -> i p d", p=128)
    xo_t = xout_ap.rearrange("(i p) d -> i p d", p=128)

    bank_ev = {i: None for i in range(8)}

    def mm_waits(bank_idx, extra):
        w = list(extra)
        if bank_ev[bank_idx] is not None:
            w.append(bank_ev[bank_idx])
        return [x for x in w if x is not None]

    m_gn = S.emit("sync", lambda e: e.dma_start(out=osq[0:1, 0:256], in_=W["gnw0"][:, :]), dma=True)
    m_mm = S.emit("tensor", lambda e: e.matmul(psum[7][:, 0:256], c["ones1f"][:], osq[0:1, 0:256],
                                               start=True, stop=True),
                  waits=[m_gn] + list(consts_ms), inc=1)
    m_gnb = S.emit("vector", lambda e: e.tensor_copy(gnb[:], psum[7][:, 0:256]),
                   waits=[m_mm], inc=1)
    bank_ev[7] = m_gnb
    m_dz = S.emit("gpsimd", lambda e: e.memset(delta[:], 0.0), inc=1)

    cons_m5 = {}
    tp_last = {}
    tp_ev = []
    ch_ready = {}
    stage = {}

    def ln_stats(i):
        w = []
        if i - 2 in cons_m5:
            w += cons_m5[i - 2]
        mld = S.emit("sync", lambda e, i=i: e.dma_start(out=xch[i % 2][:], in_=x_t[i]),
                     waits=w, dma=True)
        slc = lambda name, i=i: stats[name][:, i:i + 1]
        m1 = S.emit("scalar", lambda e, i=i, slc=slc: e.activation(
            xsq, xch[i % 2][:], AF.Square, accum_out=slc("sumsq")), waits=[mld], inc=1)
        mrd = S.emit("vector", lambda e, i=i, slc=slc: e.reduce_sum(
            slc("s").unsqueeze(2), xch[i % 2][:].rearrange("p (a b) -> p a b", a=1), axis=AX.X),
            waits=[mld], inc=1)
        mmm = S.emit("vector", lambda e, slc=slc: e.tensor_scalar_mul(slc("m"), slc("s"), 1.0 / 1024.0),
               self_waits=[mrd], inc=1)
        mv1 = S.emit("vector", lambda e, slc=slc: e.tensor_scalar_mul(slc("v1"), slc("sumsq"), 1.0 / 1024.0),
               waits=[m1], inc=1)
        m3 = S.emit("vector", lambda e, slc=slc: e.scalar_tensor_tensor(
            slc("mvar"), slc("m"), slc("m"), slc("v1"), OP.mult, OP.subtract),
            self_waits=[mmm, mv1], inc=1)
        mlnv = S.emit("scalar", lambda e, slc=slc: e.activation(slc("lnv"), slc("mvar"), AF.Ln,
                                                         bias=c["epsb"][:], scale=-1.0), waits=[m3], inc=1)
        m4 = S.emit("scalar", lambda e, slc=slc: e.activation(slc("rstd"), slc("lnv"), AF.Exp,
                                                              scale=-0.5), self_waits=[mlnv], inc=1)
        stage[i] = (slc, m1, m4)

    def ln_apply(i):
        slc, m1, m4 = stage[i]
        after = [tp_last[i - 1]] if i - 1 in tp_last else []
        mnm = S.emit("vector", lambda e, slc=slc: e.tensor_mul(slc("nm"), slc("m"), slc("rstd")),
               waits=[m4], inc=1)
        m5 = S.emit("vector", lambda e, i=i, slc=slc: e.tensor_scalar(
            xlns[i % 2][:], xch[i % 2][:], slc("rstd"), slc("nm"), OP.mult, OP.subtract),
            waits=after, self_waits=[mnm], inc=1)
        cons_m5[i] = [m5, m1]
        for grp in range(2):
            bidx = 2 + grp
            pbt = psum[bidx].bitcast(BF16)
            mts = []
            for k in range(4):
                cdx = grp * 4 + k
                slot = pbt[:, k * 128:(k + 1) * 128]
                tw = [m5] + ([bank_ev[bidx]] if bank_ev[bidx] else [])
                mt = S.emit("tensor", lambda e, s=slot, i=i, cdx=cdx: e.transpose(
                    s, xlns[i % 2][:, cdx * 128:(cdx + 1) * 128], c["identb"][:]),
                    waits=tw if k == 0 else [], inc=(1 if k == 3 else 0))
                mts.append(mt)
            for k in range(4):
                cdx = grp * 4 + k
                slot = pbt[:, k * 128:(k + 1) * 128]
                dst = xln_fm[:, cdx * 2048 + i * 128: cdx * 2048 + (i + 1) * 128]
                mev = S.emit("scalar", lambda e, s=slot, d=dst: e.copy(d, s),
                             waits=[mts[-1]], inc=1)
                tp_ev.append(mev)
            bank_ev[bidx] = tp_ev[-1]
            tp_last[i] = mts[-1]
        ch_ready[i] = tp_ev[-1]

    for i in range(TCH + 2):
        if i - 2 >= 0:
            ln_apply(i - 2)
        if i < TCH:
            ln_stats(i)
    xln_ready = tp_ev[-1]
    tb_ready = {tb: ch_ready[4 * tb + 3] for tb in range(4)}

    dms = {}
    head_end = {}
    p2_ms = {}
    prev_scores_done = None
    prev_o_last_mm = None
    prev_rope_k = []
    vmm_last = {}
    gmm_last = {}
    womm_last = {}
    delta_ms = {}
    Bms = {}
    for h in range(NH):
        p = h % 2
        wdep = [head_end.get(h - 2)]
        rdep = [p2_ms.get(h - 1)] + (prev_rope_k[:1] if prev_rope_k else [])
        for nm, buf, src, hsl, wv_ in (
                ("wq", wq_s[p], W["wq0"], (h * 128, (h + 1) * 128), wdep),
                ("wk", wk_s[p], W["wk0"], (h * 128, (h + 1) * 128), wdep),
                ("wv", wv_s[p], W["wv0"], (h * 256, (h + 1) * 256), [vmm_last.get(h - 1)]),
                ("wg", wg_s[p], W["wg0"], (h * 256, (h + 1) * 256), [gmm_last.get(h - 1)])):
            dms[(nm, h)] = S.emit("sync", lambda e, buf=buf, src=src, hsl=hsl: e.dma_start(
                out=buf[:].rearrange("p (c f) -> p c f", c=8),
                in_=src.rearrange("(c p) f -> p c f", p=128)[:, :, hsl[0]:hsl[1]]),
                waits=[x for x in wv_ if x], dma=True)

        for nm, buf, src in (("rq", rq_t[p], W["ropeq"]), ("rk", rk_t[p], W["ropek"])):
            dms[(nm, h)] = S.emit("sync", lambda e, buf=buf, src=src, h=h: e.dma_start(
                out=buf[:].rearrange("p (c f) -> p c f", c=2),
                in_=src.rearrange("(h c p) f -> p h c f", c=2, p=128)[:, h, :, :]),
                waits=[x for x in rdep if x], dma=True)

        rope = {}
        for which, w_s, r_t, outbf in (("q", wq_s[p], rq_t[p], qh), ("k", wk_s[p], rk_t[p], kh)):
            for tb in range(4):
                bidx = tb % 2
                extra = [dms[("w" + which, h)], tb_ready[tb]]
                for cdx in range(DCH):
                    m = S.emit("tensor", lambda e, bidx=bidx, cdx=cdx, tb=tb, w_s=w_s: e.matmul(
                        psum[bidx][:, :], w_s[:, cdx * 128:(cdx + 1) * 128],
                        xln_fm[:, cdx * 2048 + tb * 512: cdx * 2048 + (tb + 1) * 512],
                        start=(cdx == 0), stop=(cdx == DCH - 1)),
                        waits=mm_waits(bidx, extra) if cdx == 0 else [],
                        inc=(1 if cdx == DCH - 1 else 0))
                evw = [m]
                if which == "k" and tb == 0:
                    evw += [rope[("p1", "q")], rope[("sw", "q")]]
                if which == "q" and tb == 0 and prev_rope_k:
                    evw += prev_rope_k[:2]
                mev = S.emit("scalar", lambda e, tb=tb, bidx=bidx: e.copy(
                    raw[:, tb * 512:(tb + 1) * 512], psum[bidx][:, :]), waits=evw, inc=1)
                bank_ev[bidx] = mev
                rope[("ev", which, tb)] = mev
            last_ev = rope[("ev", which, 3)]
            swdep = [last_ev]
            if which == "k":
                swdep.append(rope[("sum", "q")])
            elif prev_rope_k:
                swdep.append(prev_rope_k[2])
            S.emit("gpsimd", lambda e: e.dma_start(out=swb[0:64, :], in_=raw[64:128, :]),
                   waits=swdep, dma=True)
            msw2 = S.emit("gpsimd", lambda e: e.dma_start(out=swb[64:128, :], in_=raw[0:64, :]),
                          dma=True)
            rope[("sw", which)] = msw2
            p1dep = [last_ev, dms[("r" + which, h)]]
            mp1 = S.emit("vector", lambda e, r_t=r_t: e.tensor_mul(p1b[:], raw[:], r_t[:, 0:2048]),
                         waits=p1dep, inc=1)
            rope[("p1", which)] = mp1
            mp2 = S.emit("gpsimd", lambda e, r_t=r_t: e.tensor_mul(swb[:], swb[:], r_t[:, 2048:4096]),
                         waits=[dms[("r" + which, h)]], self_waits=[msw2], inc=1)
            sumdep = [mp1, mp2]
            if prev_scores_done is not None:
                sumdep.append(prev_scores_done)
            msum = S.emit("vector", lambda e, outbf=outbf: e.tensor_add(outbf[:], p1b[:], swb[:]),
                          waits=sumdep, inc=1)
            rope[("sum", which)] = msum
        p2_ms[h] = mp2
        prev_rope_k = [rope[("p1", "k")], rope[("sw", "k")], rope[("sum", "k")]]

        vev = []
        for i in range(TCH):
            bidx = 2 + (i % 2)
            extra = [dms[("wv", h)], ch_ready[i]]
            for cdx in range(DCH):
                m = S.emit("tensor", lambda e, i=i, cdx=cdx, bidx=bidx, p=p: e.matmul(
                    psum[bidx][:, 0:256],
                    xln_fm[:, cdx * 2048 + i * 128: cdx * 2048 + (i + 1) * 128],
                    wv_s[p][:, cdx * 256:(cdx + 1) * 256],
                    start=(cdx == 0), stop=(cdx == DCH - 1)),
                    waits=mm_waits(bidx, extra) if cdx == 0 else [],
                    inc=(1 if cdx == DCH - 1 else 0))
            mev = S.emit("scalar", lambda e, i=i, bidx=bidx: e.copy(
                vh[:, i * 256:(i + 1) * 256], psum[bidx][:, 0:256]),
                waits=[m], inc=1)
            bank_ev[bidx] = mev
            vev.append(mev)
        v_ready = vev[-1]
        vmm_last[h] = m

        def emit_stage_c(hh):
            hhp = hh % 2
            pw = hh % 2
            for grp in range(8):
                bidx = 2 + (grp % 2)
                pbt = psum[bidx].bitcast(BF16)
                mts = []
                for k2 in range(4):
                    ti = grp * 4 + k2
                    qt, c2 = ti // 2, ti % 2
                    tw = [Bms[hh]] + ([bank_ev[bidx]] if bank_ev[bidx] else [])
                    mt = S.emit("tensor", lambda e, s=pbt[:, k2 * 128: (k2 + 1) * 128],
                                qt=qt, c2=c2, hhp=hhp: e.transpose(
                        s, o_sb[hhp][:, qt * 256 + c2 * 128: qt * 256 + (c2 + 1) * 128],
                        c["identb"][:]),
                        waits=tw if k2 == 0 else [], inc=(1 if k2 == 3 else 0))
                    mts.append(mt)
                for k2 in range(4):
                    ti = grp * 4 + k2
                    qt, c2 = ti // 2, ti % 2
                    mev = S.emit("vector", lambda e, s=pbt[:, k2 * 128: (k2 + 1) * 128],
                                 qt=qt, c2=c2: e.tensor_copy(
                        ofm[:, c2 * 2048 + qt * 128: c2 * 2048 + (qt + 1) * 128], s),
                        waits=[mts[-1]], inc=1)
                bank_ev[bidx] = mev
            ofm_ready = mev
            for qt in range(TCH):
                for half in range(2):
                    wb = 6 + half
                    www = [ofm_ready, dms[("wo", hh)]]
                    if bank_ev[wb] is not None:
                        www.append(bank_ev[wb])
                    for c2 in range(2):
                        mwo = S.emit("tensor", lambda e, half=half, c2=c2, qt=qt, pw=pw, wb=wb: e.matmul(
                            psum[wb][:, :], ofm[:, c2 * 2048 + qt * 128: c2 * 2048 + (qt + 1) * 128],
                            wo_s[pw][:, c2 * 1024 + half * 512: c2 * 1024 + (half + 1) * 512],
                            start=(c2 == 0), stop=(c2 == 1)),
                            waits=[x for x in www if x] if (c2 == 0) else [],
                            inc=(1 if c2 == 1 else 0))
                    md = S.emit("vector", lambda e, half=half, qt=qt, wb=wb, hh=hh: e.tensor_tensor(
                        delta[:, qt * 1024 + half * 512: qt * 1024 + (half + 1) * 512],
                        delta[:, qt * 1024 + half * 512: qt * 1024 + (half + 1) * 512],
                        psum[wb][:, :], OP.add), waits=[mwo] + ([m_dz] if hh == 0 else []), inc=1)
                    bank_ev[wb] = md
                delta_ms[(hh, qt)] = md
            womm_last[hh] = mwo
            head_end[hh] = delta_ms[(hh, 15)]

        hp = h % 2
        warB = [Bms.get(h - 2)]
        for qt in range(TCH):
            gb = 4 + (qt % 2)
            gw = [dms[("wg", h)], ch_ready[qt]] + warB
            if bank_ev[gb] is not None:
                gw.append(bank_ev[gb])
            for cdx in range(DCH):
                mg = S.emit("tensor", lambda e, gb=gb, cdx=cdx, qt=qt, p=p: e.matmul(
                    psum[gb][:, 0:256], xln_fm[:, cdx * 2048 + qt * 128: cdx * 2048 + (qt + 1) * 128],
                    wg_s[p][:, cdx * 256:(cdx + 1) * 256],
                    start=(cdx == 0), stop=(cdx == DCH - 1)),
                    waits=[x for x in gw if x] if cdx == 0 else [],
                    inc=(1 if cdx == DCH - 1 else 0))
            mgev = S.emit("scalar", lambda e, gb=gb, qt=qt, hp=hp: e.activation(
                gsil[hp][:, qt * 256:(qt + 1) * 256], psum[gb][:, 0:256], AF.Silu),
                waits=[mg] + ([Bms[h - 1]] if (h - 1) in Bms and qt == 0 else []), inc=1)
            bank_ev[gb] = mgev
            gmm_last[h] = mg
        g_all = mgev

        if not L0A_NOPIPE and h > 0:
            emit_stage_c(h - 1)
        if not L0A_NOPIPE:
            dms[("wo", h)] = S.emit("sync", lambda e, p=p, h=h: e.dma_start(
                out=wo_s[p][:].rearrange("p (c f) -> p c f", c=2),
                in_=W["wo0"].rearrange("(c p) f -> p c f", p=128)[:, h * 2:(h + 1) * 2, :]),
                waits=[x for x in [womm_last.get(h - 1)] if x], dma=True)

        st_ev = {}
        o_ev = {}
        for j in range(4):
            for m in range(4 * j + 4):
                bidx = m % 2
                r = m - 4 * j
                extra = [rope[("sum", "q")], rope[("sum", "k")]]
                mm = S.emit("tensor", lambda e, bidx=bidx, m=m, j=j: e.matmul(
                    psum[bidx][:, :], kh[:, m * 128:(m + 1) * 128], qh[:, j * 512:(j + 1) * 512],
                    start=True, stop=True), waits=mm_waits(bidx, extra), inc=1)
                if r >= 0:
                    mev = S.emit("vector", lambda e, bidx=bidx, m=m, r=r: e.tensor_tensor(
                        St[:, m * 512:(m + 1) * 512], psum[bidx][:, :],
                        c["maskSt"][:, r * 512:(r + 1) * 512], OP.mult), waits=[mm], inc=1)
                elif m % 2:
                    mev = S.emit("scalar", lambda e, bidx=bidx, m=m: e.copy(
                        St[:, m * 512:(m + 1) * 512], psum[bidx][:, :]), waits=[mm], inc=1)
                else:
                    mev = S.emit("vector", lambda e, bidx=bidx, m=m: e.tensor_copy(
                        St[:, m * 512:(m + 1) * 512], psum[bidx][:, :]), waits=[mm], inc=1)
                bank_ev[bidx] = mev
                st_ev[(m, j)] = mev
                prev_scores_done = mev
            for qt in range(4 * j, 4 * j + 4):
                ob = 2 + (qt % 2)
                gb = 4 + (qt % 2)
                ow = [st_ev[(m2, j)] for m2 in range(qt + 1)] + [v_ready] + warB
                if bank_ev[ob] is not None:
                    ow.append(bank_ev[ob])
                for m2 in range(qt + 1):
                    mo = S.emit("tensor", lambda e, ob=ob, m2=m2, j=j, qt=qt: e.matmul(
                        psum[ob][:, 0:256], St[:, m2 * 512 + (qt % 4) * 128: m2 * 512 + (qt % 4 + 1) * 128],
                        vh[:, m2 * 256:(m2 + 1) * 256],
                        start=(m2 == 0), stop=(m2 == qt)),
                        waits=[x for x in ow if x] if m2 == 0 else [],
                        inc=(1 if m2 == qt else 0))
                if qt % 2 == 0:
                    moev = S.emit("vector", lambda e, ob=ob, qt=qt, hp=hp: e.tensor_copy(
                        o_sb[hp][:, qt * 256:(qt + 1) * 256], psum[ob][:, 0:256]), waits=[mo], inc=1)
                else:
                    moev = S.emit("scalar", lambda e, ob=ob, qt=qt, hp=hp: e.copy(
                        o_sb[hp][:, qt * 256:(qt + 1) * 256], psum[ob][:, 0:256]), waits=[mo], inc=1)
                bank_ev[ob] = moev
                o_ev[qt] = moev
        o_all = o_ev[TCH - 1]

        mrs = None
        for qt in range(TCH):
            mrs = S.emit("scalar", lambda e, qt=qt, hp=hp: e.activation(
                osq[:], o_sb[hp][:, qt * 256:(qt + 1) * 256], AF.Square,
                accum_out=ost[:, qt:qt + 1]),
                waits=[o_ev[qt]] if qt % 2 == 0 else [], self_waits=[mrs] if mrs else [], inc=1)
        mln3 = S.emit("scalar", lambda e: e.activation(ost[:, 16:32], ost[:, 0:16], AF.Ln,
                                                       bias=c["epsb"][:], scale=1.0 / 256.0),
                      self_waits=[mrs], inc=1)
        mrstd = S.emit("scalar", lambda e: e.activation(ost[:, 32:48], ost[:, 16:32], AF.Exp,
                                                        scale=-0.5), self_waits=[mln3], inc=1)
        mt1 = S.emit("vector", lambda e, hp=hp: e.tensor_tensor(
            gsil[hp][:].rearrange("p (q f) -> p q f", f=256),
            gsil[hp][:].rearrange("p (q f) -> p q f", f=256),
            gnb[:].unsqueeze(1).broadcast_to([128, 16, 256]), OP.mult),
            waits=[g_all, m_gnb], inc=1)
        mon1 = S.emit("vector", lambda e, hp=hp: e.tensor_tensor(
            o_sb[hp][:].rearrange("p (q f) -> p q f", f=256),
            o_sb[hp][:].rearrange("p (q f) -> p q f", f=256),
            ost[:, 32:48].unsqueeze(2).broadcast_to([128, 16, 256]), OP.mult),
            waits=[mrstd, o_all], inc=1)
        monbf = S.emit("vector", lambda e, hp=hp: e.tensor_mul(
            o_sb[hp][:], o_sb[hp][:], gsil[hp][:]), self_waits=[mt1, mon1], inc=1)
        Bms[h] = monbf

        if L0A_NOPIPE:
            dms[("wo", h)] = S.emit("sync", lambda e, p=p, h=h: e.dma_start(
                out=wo_s[p][:].rearrange("p (c f) -> p c f", c=2),
                in_=W["wo0"].rearrange("(c p) f -> p c f", p=128)[:, h * 2:(h + 1) * 2, :]),
                waits=[x for x in [womm_last.get(h - 1)] if x], dma=True)
            emit_stage_c(h)
        elif h == NH - 1:
            emit_stage_c(h)
        prev_o_last_mm = None

    vhf = vh[:].bitcast(F32)
    stf = St[:].bitcast(F32)
    ibufs = [xch[0][:], xch[1][:], vhf[:, 0:1024], vhf[:, 1024:2048]]
    obufs = [p1b[:, 0:1024], p1b[:, 1024:2048], stf[:, 0:1024], stf[:, 1024:2048]]
    fin = []
    mads = []
    for i in range(TCH):
        w = [Bms[NH - 1]]
        if i - 4 >= 0:
            w.append(mads[i - 4])
        mld = S.emit("sync" if i % 2 == 0 else "scalar",
                     lambda e, b=ibufs[i % 4], i=i: e.dma_start(out=b, in_=x_t[i]),
                     waits=[x for x in w if x], dma=True)
        aw = [mld, delta_ms[(NH - 1, i)]] + ([fin[i - 4]] if i - 4 >= 0 else [])
        mad = S.emit("vector", lambda e, bi=ibufs[i % 4], bo=obufs[i % 4], i=i: e.tensor_tensor(
            bo, bi, delta[:, i * 1024:(i + 1) * 1024], OP.add),
            waits=aw, inc=1)
        mads.append(mad)
        mst = S.emit("gpsimd", lambda e, bo=obufs[i % 4], i=i: e.dma_start(out=xo_t[i], in_=bo),
                     waits=[mad], dma=True)
        fin.append(mst)
    dbg = dict(xln_fm=xln_fm, qh=qh, kh=kh, vh=vh, St=St, delta=delta, raw=raw, swb=swb, p1b=p1b, rq0=rq_t[0], rk0=rk_t[0], o_sb0=o_sb[0], o_sb1=o_sb[1], gsil0=gsil[0], ofm=ofm)
    dbg.update({('st_'+k): v[:] for k, v in stats.items()})
    return fin, dbg



def emit_l0_ffn(nc, S, es, c, psum, x_ap, xout_ap, W, consts_ms, core_id):
    sb = lambda n, s, d: es.enter_context(nc.sbuf_tensor(n, s, d))
    xt_parts = [nc.dram_tensor(f"l0f_xtp{j}", [1024, 1024], F32).ap() for j in range(2)]
    xt_red = [nc.dram_tensor(f"l0f_xtr{j}", [128, 1024], F32).ap() for j in range(2)]
    ym_mys = [nc.dram_tensor(f"l0f_ymy{j}", [128, 1024], BF16).ap() for j in range(2)]
    ytld_alls = [nc.dram_tensor(f"l0f_yta{j}", [1024, 1024], BF16).ap() for j in range(2)]
    sem_rs = [S.sem("l0f_rs0", es), S.sem("l0f_rs1", es)]
    sem_ag = [S.sem("l0f_ag0", es), S.sem("l0f_ag1", es)]

    xch = [sb(f"l0f_xch{i}", [128, 1024], F32) for i in range(2)]
    xsq = sb("l0f_xsq", [128, 1024], F32)
    stats = alloc_stats(nc, es, "l0f")
    stats2 = {n: sb(f"l0f2_{n}", [128, 4], F32) for n in
              ["s", "sumsq", "m", "v1", "mvar", "lnv", "rstd", "nm"]}
    phib_r = sb("l0f_phibr", [1, 2048], BF16)
    mstat = sb("l0f_mstat", [128, 32], F32)
    Ebuf = sb("l0f_E", [128, 16 * 2048], BF16)
    Dt = [sb(f"l0f_Dt{i}", [128, 128], BF16) for i in range(2)]
    xtp_s = [sb(f"l0f_xtps{i}", [128, 1024], F32) for i in range(2)]
    expb_r = [sb(f"l0f_expbr{i}", [1, 1024], F32) for i in range(2)]
    arena = sb("l0f_arena", [128, 24576], F32)
    arb = arena[:].bitcast(BF16)
    phiw_s = arb[:, 0:16384]
    logit_sb_ = [arena[:, 8192:10240], arena[:, 20480:22528]]
    xlnfm_t = [arb[:, 20480 + i * 1024: 20480 + (i + 1) * 1024] for i in range(2)]
    xln_tm = arb[:, 22528:38912]
    recD = arena[:, 19456:20480]
    ew_s = [arb[:, i * 8192:(i + 1) * 8192] for i in range(2)]
    ytld = arb[:, 16384:32768]
    ym32h = [arb[0:32, 32768:36864], arb[0:32, 45056:49152]]
    sumc_sb = arena[0:32, 16384:18432]
    recC_pt = sumc_sb
    recC_b = arena[:, 18432:20480]
    X_T = arb[:, 40960:43008]
    Xl = [arb[:, 43008 + i * 1024: 43008 + (i + 1) * 1024] for i in range(2)]
    xrd = [arena[:, 22528 + i * 1024: 22528 + (i + 1) * 1024] for i in range(2)]

    x_t = x_ap.rearrange("(i p) d -> i p d", p=128)
    xo_t = xout_ap.rearrange("(i p) d -> i p d", p=128)
    bank_ev = {i: None for i in range(8)}

    def mm_waits(bank_idx, extra):
        w = list(extra)
        if bank_ev[bank_idx] is not None:
            w.append(bank_ev[bank_idx])
        return [x for x in w if x is not None]

    m_phiw = S.emit("sync", lambda e: e.dma_start(
        out=phiw_s.rearrange("p (c f) -> p c f", c=8)[:, 0:4, :],
        in_=W["phiwT"].rearrange("(c p) f -> p c f", p=128)[:, 0:4, :]), dma=True)
    m_phiw2 = S.emit("gpsimd", lambda e: e.dma_start(
        out=phiw_s.rearrange("p (c f) -> p c f", c=8)[:, 4:8, :],
        in_=W["phiwT"].rearrange("(c p) f -> p c f", p=128)[:, 4:8, :]), dma=True)
    m_phibr = S.emit("sync", lambda e: e.dma_start(out=phib_r[:], in_=W["phib"][:, :]), dma=True)


    cons_ms = {}
    tp_last = {}
    Ems = {}
    Dms = {}
    logit_ev = {}
    stage = {}

    def ln_stats(i):
        w = []
        if i - 2 in cons_ms:
            w += cons_ms[i - 2]
        mld = S.emit("gpsimd" if i % 2 == 0 else "sync",
                     lambda e, i=i: e.dma_start(out=xch[i % 2][:], in_=x_t[i]),
                     waits=w, dma=True)
        slc = lambda name, i=i: stats[name][:, i:i + 1]
        m1 = S.emit("scalar", lambda e, i=i, slc=slc: e.activation(
            xsq[:], xch[i % 2][:], AF.Square, accum_out=slc("sumsq")), waits=[mld], inc=1)
        after = [tp_last[i - 2]] if i - 2 in tp_last else []
        mrd = S.emit("vector", lambda e, i=i, slc=slc: e.reduce_sum(
            slc("s").unsqueeze(2), xch[i % 2][:].rearrange("p (a b) -> p a b", a=1), axis=AX.X),
            waits=[mld] + after, inc=1)
        mmm = S.emit("vector", lambda e, slc=slc: e.tensor_scalar_mul(slc("m"), slc("s"), 1.0 / 1024.0),
               self_waits=[mrd], inc=1)
        mv1 = S.emit("vector", lambda e, slc=slc: e.tensor_scalar_mul(slc("v1"), slc("sumsq"), 1.0 / 1024.0),
               waits=[m1], inc=1)
        m3 = S.emit("vector", lambda e, slc=slc: e.scalar_tensor_tensor(
            slc("mvar"), slc("m"), slc("m"), slc("v1"), OP.mult, OP.subtract),
            self_waits=[mmm, mv1], inc=1)
        mlnv = S.emit("scalar", lambda e, slc=slc: e.activation(slc("lnv"), slc("mvar"), AF.Ln,
                                                         bias=c["epsb"][:], scale=-1.0),
               waits=[m3], inc=1)
        m4 = S.emit("scalar", lambda e, slc=slc: e.activation(slc("rstd"), slc("lnv"), AF.Exp,
                                                              scale=-0.5), self_waits=[mlnv], inc=1)
        stage[i] = (slc, m1, m4)

    def ln_apply(i):
        slc, m1, m4 = stage[i]
        mnm = S.emit("vector", lambda e, slc=slc: e.tensor_mul(slc("nm"), slc("m"), slc("rstd")),
               waits=[m4], inc=1)
        m5 = S.emit("vector", lambda e, i=i, slc=slc: e.tensor_scalar(
            xln_tm[:, i * 1024:(i + 1) * 1024], xch[i % 2][:], slc("rstd"), slc("nm"),
            OP.mult, OP.subtract), self_waits=[mnm], inc=1)
        cons_ms[i] = [m5, m1]
        tp_ev_local = []
        for grp in range(2):
            bidx = 2 + grp
            pbt = psum[bidx].bitcast(BF16)
            mts = []
            for k in range(4):
                cdx = grp * 4 + k
                tw = [m5] + ([bank_ev[bidx]] if bank_ev[bidx] else [])
                mt = S.emit("tensor", lambda e, s=pbt[:, k * 128:(k + 1) * 128], i=i, cdx=cdx: e.transpose(
                    s, xln_tm[:, i * 1024 + cdx * 128: i * 1024 + (cdx + 1) * 128], c["identb"][:]),
                    waits=tw if k == 0 else [], inc=(1 if k == 3 else 0))
                mts.append(mt)
            evw = [cons_ms[i - 2][0]] if False else []
            for k in range(4):
                cdx = grp * 4 + k
                dst = xlnfm_t[i % 2][:, cdx * 128:(cdx + 1) * 128]
                mev = S.emit("scalar", lambda e, s=pbt[:, k * 128:(k + 1) * 128], d=dst: e.copy(d, s),
                             waits=[mts[-1]] + ([logit_ev.get(i - 2)] if (i - 2) in logit_ev and k == 0 else []),
                             inc=1)
                tp_ev_local.append(mev)
            bank_ev[bidx] = tp_ev_local[-1]
            tp_last[i] = mts[-1]
        fm_ready = tp_ev_local[-1]
        for eb in range(4):
            bidx = eb % 2
            extra = [fm_ready, m_phiw, m_phiw2, m_phibr] + list(consts_ms)
            for cdx in range(DCH):
                m = S.emit("tensor", lambda e, bidx=bidx, cdx=cdx, eb=eb, i=i: e.matmul(
                    psum[bidx][:, :], xlnfm_t[i % 2][:, cdx * 128:(cdx + 1) * 128],
                    phiw_s[:, cdx * 2048 + eb * 512: cdx * 2048 + (eb + 1) * 512],
                    start=(cdx == 0), stop=False),
                    waits=mm_waits(bidx, extra) if cdx == 0 else [], inc=0)
            m = S.emit("tensor", lambda e, bidx=bidx, eb=eb: e.matmul(
                psum[bidx][:, :], c["ones1b"][:], phib_r[0:1, eb * 512:(eb + 1) * 512],
                start=False, stop=True), inc=1)
            mev = S.emit("vector", lambda e, bidx=bidx, eb=eb, i=i: e.tensor_copy(
                logit_sb_[i % 2][:, eb * 512:(eb + 1) * 512], psum[bidx][:, :]),
                waits=[m] + ([Ems[i - 2]] if (i - 2) in Ems and eb == 0 else []), inc=1)
            bank_ev[bidx] = mev
        logit_ev[i] = mev
        mc = lambda k2, i=i: mstat[:, i * 2:i * 2 + 1] if k2 == 0 else mstat[:, i * 2 + 1:i * 2 + 2]
        mrx = S.emit("vector", lambda e, i=i, mc=mc: e.reduce_max(
            mc(0).unsqueeze(2), logit_sb_[i % 2].rearrange("p (a b) -> p a b", a=1), axis=AX.X),
            waits=[mev], inc=1)
        mneg = S.emit("vector", lambda e, mc=mc: e.tensor_scalar_mul(mc(1), mc(0), -1.0),
                      self_waits=[mrx], inc=1)
        mE = S.emit("scalar", lambda e, i=i, mc=mc: e.activation(
            Ebuf[:, i * 2048:(i + 1) * 2048], logit_sb_[i % 2], AF.Exp, bias=mc(1)),
            waits=[mneg], inc=1)
        Ems[i] = mE
        msD = S.emit("vector", lambda e, i=i: e.reduce_sum(
            recD[:, i * 64:(i + 1) * 64].unsqueeze(2),
            Ebuf[:, i * 2048:(i + 1) * 2048].rearrange("p (n q) -> p n q", q=32), axis=AX.X),
            waits=[mE], inc=1)
        mrcp = S.emit("vector", lambda e, i=i: e.reciprocal(
            recD[:, i * 64:(i + 1) * 64], recD[:, i * 64:(i + 1) * 64]),
            self_waits=[msD], inc=1)
        Dms[i] = mrcp

    for i in range(TCH + 2):
        if i - 2 >= 0:
            ln_apply(i - 2)
        if i < TCH:
            ln_stats(i)

    xtp_dma = []
    dt_ms = {}
    dmm_ms = {}
    for ec in range(TCH):
        for i in range(TCH):
            dw = [Dms[i], Ems[i]]
            if (ec, i - 2) in dmm_ms:
                dw.append(dmm_ms[(ec, i - 2)])
            elif ec > 0 and (ec - 1, TCH - 2 + i) in dmm_ms:
                dw.append(dmm_ms[(ec - 1, TCH - 2 + i)])
            mdt = S.emit("vector", lambda e, i=i, ec=ec: e.tensor_tensor(
                Dt[i % 2][:].rearrange("p (n q) -> p n q", q=32),
                Ebuf[:, i * 2048 + ec * 128: i * 2048 + (ec + 1) * 128].rearrange("p (n q) -> p n q", q=32),
                recD[:, i * 64 + ec * 4: i * 64 + (ec + 1) * 4].unsqueeze(2).broadcast_to([128, 4, 32]),
                OP.mult), waits=[x for x in dw if x[0] != "vector"],
                self_waits=[x for x in dw if x[0] == "vector"], inc=1)
            dt_ms[(ec, i)] = mdt
            for dh in range(2):
                bidx = 4 + dh
                extra = [mdt, cons_ms[15][0]]
                m = S.emit("tensor", lambda e, bidx=bidx, ec=ec, dh=dh, i=i: e.matmul(
                    psum[bidx][:, :], Dt[i % 2][:],
                    xln_tm[:, i * 1024 + dh * 512: i * 1024 + (dh + 1) * 512],
                    start=(i == 0), stop=(i == TCH - 1)),
                    waits=mm_waits(bidx, extra) if i == 0 else [mdt], inc=1)
                dmm_ms[(ec, i)] = m
        evs = []
        for dh in range(2):
            bidx = 4 + dh
            mev = S.emit("vector", lambda e, bidx=bidx, ec=ec, dh=dh: e.tensor_copy(
                xtp_s[ec % 2][:, dh * 512:(dh + 1) * 512], psum[bidx][:, :]),
                waits=[dmm_ms[(ec, TCH - 1)]] + ([xtp_dma[ec - 2]] if ec >= 2 else []), inc=1)
            bank_ev[bidx] = mev
            evs.append(mev)
        md = S.emit("sync", lambda e, ec=ec: e.dma_start(
            out=xt_parts[ec // 8].rearrange("(e2 p) d -> e2 p d", p=128)[ec % 8], in_=xtp_s[ec % 2][:]),
            waits=evs, dma=True)
        xtp_dma.append(md)

    for j in range(2):
        S.emit("gpsimd", lambda e, j=j: e.collective_compute(
            "ReduceScatter", OP.add, [[0, 1, 2, 3, 4, 5, 6, 7]],
            [xt_parts[j].opt()], [xt_red[j].opt()]),
            waits=[xtp_dma[8 * j + 7]], custom_sem=sem_rs[j])

    et_ev = {}
    for i in range(TCH):
        for grp in range(4):
            bidx = 2 + (grp % 2)
            pbt = psum[bidx].bitcast(BF16)
            mts = []
            for k in range(4):
                ec = grp * 4 + k
                srcsl = Ebuf[:, i * 2048 + ec * 128: i * 2048 + (ec + 1) * 128]
                tw = [dt_ms[(ec, i)], Ems[i]] + ([bank_ev[bidx]] if bank_ev[bidx] else [])
                mt = S.emit("tensor", lambda e, s=pbt[:, k * 128:(k + 1) * 128], src=srcsl: e.transpose(
                    s, src, c["identb"][:]),
                    waits=tw if k == 0 else [], inc=(1 if k == 3 else 0))
                mts.append(mt)
            for k in range(4):
                ec = grp * 4 + k
                dst = Ebuf[:, i * 2048 + ec * 128: i * 2048 + (ec + 1) * 128]
                mev = S.emit("scalar", lambda e, s=pbt[:, k * 128:(k + 1) * 128], d=dst: e.copy(d, s),
                             waits=[mts[-1]], inc=1)
            bank_ev[bidx] = mev
            et_ev[(i, grp)] = mev
    et_all = mev

    sc_ev = []
    for tb in range(4):
        bidx = 4 + tb
        extra = [et_all] + list(consts_ms)
        for i2 in range(tb * 4, tb * 4 + 4):
            for ec in range(TCH):
                m = S.emit("tensor", lambda e, bidx=bidx, i2=i2, ec=ec, tb=tb: e.matmul(
                    psum[bidx][0:32, (i2 % 4) * 128:(i2 % 4 + 1) * 128],
                    c["sel128"][:], Ebuf[:, i2 * 2048 + ec * 128: i2 * 2048 + (ec + 1) * 128],
                    start=(ec == 0), stop=(ec == TCH - 1)),
                    waits=mm_waits(bidx, extra) if (ec == 0 and i2 % 4 == 0) else [],
                    inc=(1 if ec == TCH - 1 else 0))
        mev = S.emit("vector", lambda e, bidx=bidx, tb=tb: e.tensor_copy(
            sumc_sb[:, tb * 512:(tb + 1) * 512], psum[bidx][0:32, :]),
            waits=[m, xtp_dma[-1]], inc=1)
        bank_ev[bidx] = mev
        sc_ev.append(mev)
    mrc = S.emit("vector", lambda e: e.reciprocal(recC_pt[:], sumc_sb[:]), waits=sc_ev, inc=1)
    rc_ev = []
    for tb in range(4):
        bidx = tb % 2
        m = S.emit("tensor", lambda e, bidx=bidx, tb=tb: e.matmul(
            psum[bidx][:, :], c["rep32"][:], recC_pt[:, tb * 512:(tb + 1) * 512],
            start=True, stop=True),
            waits=mm_waits(bidx, [mrc, Dms[TCH - 1]]), inc=1)
        mev = S.emit("vector", lambda e, bidx=bidx, tb=tb: e.tensor_copy(
            recC_b[:, tb * 512:(tb + 1) * 512], psum[bidx][:, :]), waits=[m], inc=1)
        bank_ev[bidx] = mev
        rc_ev.append(mev)
    xl_ms = []
    for j in range(2):
        mld = S.emit("sync", lambda e, j=j: e.dma_start(out=xrd[j][:], in_=xt_red[j][:, :]),
                     sem_waits=[(sem_rs[j], 1)], dma=True)
        mcl = S.emit("vector", lambda e, j=j: e.tensor_scalar(
            xrd[j][:], xrd[j][:], 65000.0, -33000.0, OP.min, OP.max), waits=[mld], inc=1)
        slc = lambda name, j=j: stats2[name][:, j:j + 1]
        m1 = S.emit("scalar", lambda e, j=j, slc=slc: e.activation(
            xsq[:], xrd[j][:], AF.Square, accum_out=slc("sumsq")), waits=[mcl], inc=1)
        mrd2 = S.emit("vector", lambda e, j=j, slc=slc: e.reduce_sum(
            slc("s").unsqueeze(2), xrd[j][:].rearrange("p (a b) -> p a b", a=1), axis=AX.X), inc=1)
        mmm = S.emit("vector", lambda e, slc=slc: e.tensor_scalar_mul(slc("m"), slc("s"), 1.0 / 1024.0),
               self_waits=[mrd2], inc=1)
        mv1 = S.emit("vector", lambda e, slc=slc: e.tensor_scalar_mul(slc("v1"), slc("sumsq"), 1.0 / 1024.0),
               waits=[m1], inc=1)
        m3 = S.emit("vector", lambda e, slc=slc: e.scalar_tensor_tensor(
            slc("mvar"), slc("m"), slc("m"), slc("v1"), OP.mult, OP.subtract),
            self_waits=[mmm, mv1], inc=1)
        mlnv = S.emit("scalar", lambda e, slc=slc: e.activation(slc("lnv"), slc("mvar"), AF.Ln,
                                                         bias=c["epsb"][:], scale=-1.0),
               waits=[m3], inc=1)
        m4 = S.emit("scalar", lambda e, slc=slc: e.activation(slc("rstd"), slc("lnv"), AF.Exp,
                                                              scale=-0.5), self_waits=[mlnv], inc=1)
        mnm = S.emit("vector", lambda e, slc=slc: e.tensor_mul(slc("nm"), slc("m"), slc("rstd")),
               waits=[m4], inc=1)
        m5 = S.emit("vector", lambda e, j=j, slc=slc: e.tensor_scalar(
            xsq[:], xrd[j][:], slc("rstd"), slc("nm"), OP.mult, OP.subtract),
            self_waits=[mnm], inc=1)
        mth = S.emit("scalar", lambda e, j=j: e.activation(Xl[j], xsq[:], AF.Tanh),
                     waits=[m5], inc=1)
        for grp in range(2):
            bidx = 2 + grp
            pbt = psum[bidx].bitcast(BF16)
            mts = []
            for k in range(4):
                cdx = grp * 4 + k
                tw = [mth] + ([bank_ev[bidx]] if bank_ev[bidx] else [])
                mt = S.emit("tensor", lambda e, s=pbt[:, k * 128:(k + 1) * 128], j=j, cdx=cdx: e.transpose(
                    s, Xl[j][:, cdx * 128:(cdx + 1) * 128], c["identb"][:]),
                    waits=tw if k == 0 else [], inc=(1 if k == 3 else 0))
                mts.append(mt)
            for k in range(4):
                cdx = grp * 4 + k
                dst = X_T[:, cdx * 256 + j * 128: cdx * 256 + (j + 1) * 128]
                mev = S.emit("scalar", lambda e, s=pbt[:, k * 128:(k + 1) * 128], d=dst: e.copy(d, s),
                             waits=[mts[-1]], inc=1)
            bank_ev[bidx] = mev
        xl_ms.append(mev)
    xt_ready = xl_ms[-1]

    ew_dma = {}
    eb_dma = {}
    ym_ev = []
    for n in range(8):
        p = n % 2
        wdep = [ym_ev[-3]] if len(ym_ev) >= 3 else []
        ew_dma[n] = S.emit("sync", lambda e, n=n, p=p: e.dma_start(
            out=ew_s[p].rearrange("p (c f) -> p c f", c=8)[:, 0:4, :],
            in_=W["expw"].rearrange("n (c p) H -> n p c H", p=128)[n, :, 0:4, :]),
            waits=wdep, dma=True)
        ew_dma[(n, 1)] = S.emit("gpsimd", lambda e, n=n, p=p: e.dma_start(
            out=ew_s[p].rearrange("p (c f) -> p c f", c=8)[:, 4:8, :],
            in_=W["expw"].rearrange("n (c p) H -> n p c H", p=128)[n, :, 4:8, :]),
            waits=wdep, dma=True)
        eb_dma[n] = S.emit("sync", lambda e, n=n, p=p: e.dma_start(
            out=expb_r[p][:], in_=W["expb"][0:1, n * 1024:(n + 1) * 1024]),
            waits=wdep, dma=True)
        for Hh in range(2):
            bidx = 4 + ((n * 2 + Hh) % 2)
            extra = [ew_dma[n], ew_dma[(n, 1)], xl_ms[0] if n < 4 else xl_ms[1], eb_dma[n]]
            for cdx in range(DCH):
                m = S.emit("tensor", lambda e, bidx=bidx, n=n, Hh=Hh, cdx=cdx, p=p: e.matmul(
                    psum[bidx][0:32, :],
                    X_T[:, cdx * 256 + (n % 4) * 32 + (n // 4) * 128: cdx * 256 + (n % 4) * 32 + (n // 4) * 128 + 32],
                    ew_s[p][:, cdx * 1024 + Hh * 512: cdx * 1024 + (Hh + 1) * 512],
                    start=(cdx == 0), stop=False),
                    waits=mm_waits(bidx, extra) if cdx == 0 else [], inc=0)
            m = S.emit("tensor", lambda e, bidx=bidx, n=n, Hh=Hh, p=p: e.matmul(
                psum[bidx][0:32, :], c["ones32f"][:],
                expb_r[p][0:1, Hh * 512:(Hh + 1) * 512],
                start=False, stop=True), inc=1)
            mev = S.emit("vector", lambda e, bidx=bidx, n=n, Hh=Hh: e.tensor_copy(
                ym32h[n // 4][:, (n % 4) * 1024 + Hh * 512: (n % 4) * 1024 + (Hh + 1) * 512],
                psum[bidx][0:32, :]),
                waits=[m] + ([mrc] if n == 0 else []), inc=1)
            bank_ev[bidx] = mev
            ym_ev.append(mev)
    cs_ms = {}
    for i in range(TCH):
        mcs = S.emit("vector", lambda e, i=i: e.tensor_tensor(
            Ebuf[:, i * 2048:(i + 1) * 2048].rearrange("p (ec t) -> p ec t", t=128),
            Ebuf[:, i * 2048:(i + 1) * 2048].rearrange("p (ec t) -> p ec t", t=128),
            recC_b[:, i * 128:(i + 1) * 128].unsqueeze(1).broadcast_to([128, 16, 128]), OP.mult),
            waits=rc_ev, inc=1)
        cs_ms[i] = mcs
    m_ym_dmas = []
    for j in range(2):
        m_ym_dma = S.emit("scalar", lambda e, j=j: e.dma_start(
            out=ym_mys[j].rearrange("(n p) H -> p n H", p=32),
            in_=ym32h[j].rearrange("p (n H) -> p n H", n=4)),
            waits=ym_ev[j * 8:(j + 1) * 8], dma=True)
        m_ym_dmas.append(m_ym_dma)
        S.emit("gpsimd", lambda e, j=j: e.collective_compute(
            "AllGather", OP.bypass, [[0, 1, 2, 3, 4, 5, 6, 7]],
            [ym_mys[j].opt()], [ytld_alls[j].opt()]),
            waits=[m_ym_dma], custom_sem=sem_ag[j])
    m_yts = []
    for j in range(2):
        m_yt = S.emit("scalar", lambda e, j=j: e.dma_start(
            out=ytld.rearrange("p (ec H) -> p ec H", ec=16)[:, j * 8:(j + 1) * 8, :],
            in_=ytld_alls[j].rearrange("(ec p) H -> p ec H", p=128)),
            sem_waits=[(sem_ag[j], 1)], dma=True)
        m_yts.append(m_yt)
    m_yt = m_yts[0]
    fin = []
    xo_ev = {}
    for i in range(TCH):
        w = [cs_ms[i], cons_ms[15][0], cons_ms[15][1]]
        if i - 2 >= 0:
            w.append(xo_ev[i - 2])
        mld = S.emit("sync", lambda e, i=i: e.dma_start(out=xch[i % 2][:], in_=x_t[i]),
                     waits=[x for x in w if x], dma=True)
        evs = []
        for Hh in range(2):
            bidx = 6 + Hh
            extra = [cs_ms[i], m_yt]
            for ec in range(TCH):
                m = S.emit("tensor", lambda e, bidx=bidx, i=i, ec=ec, Hh=Hh: e.matmul(
                    psum[bidx][:, :], Ebuf[:, i * 2048 + ec * 128: i * 2048 + (ec + 1) * 128],
                    ytld[:, ec * 1024 + Hh * 512: ec * 1024 + (Hh + 1) * 512],
                    start=(ec == 0), stop=(ec == TCH - 1)),
                    waits=(mm_waits(bidx, extra) if ec == 0 else
                           ([m_yts[1]] if ec == 8 else [])),
                    inc=(1 if ec == TCH - 1 else 0))
            mev = S.emit("vector", lambda e, bidx=bidx, i=i, Hh=Hh: e.tensor_tensor(
                xtp_s[i % 2][:, Hh * 512:(Hh + 1) * 512], xch[i % 2][:, Hh * 512:(Hh + 1) * 512],
                psum[bidx][:, :], OP.add),
                waits=[m, mld] + ([xo_ev[i - 2]] if i - 2 in xo_ev else [xtp_dma[-1]]), inc=1)
            bank_ev[bidx] = mev
            evs.append(mev)
        mst = S.emit("gpsimd", lambda e, i=i: e.dma_start(out=xo_t[i], in_=xtp_s[i % 2][:]),
                     waits=evs, dma=True)
        xo_ev[i] = mst
        fin.append(mst)
    dbg = dict(E=Ebuf, recD=recD, recC_b=recC_b, xln_tm=xln_tm, ytld=ytld,
               X_T=X_T, ym32=ym32h[0])
    return fin, dbg



def emit_l1_attn(nc, S, es, c, psum, x_ap, xout_ap, W, consts_ms):
    sb = lambda n, s, d: es.enter_context(nc.sbuf_tensor(n, s, d))
    xch = [sb(f"l1a_xch{i}", [128, 1024], F32) for i in range(2)]
    xsq = sb("l1a_xsq", [128, 1024], F32)
    stats = alloc_stats(nc, es, "l1a")
    xlns = [sb("l1a_xlns0", [128, 1024], BF16)] * 2
    xln_fm = sb("l1a_xlnfm", [128, 8 * 2048], BF16)
    y_fm = xln_fm
    q_fm = sb("l1a_qfm", [128, 8 * 2048], BF16)
    k_fm = sb("l1a_kfm", [128, 8 * 2048], BF16)
    v_tm = sb("l1a_vtm", [128, 16 * 1024], BF16)
    lcs_tm = sb("l1a_lcs", [128, 16 * 16], F32)
    arenaA = sb("l1a_arA", [128, 4096], F32)
    arAb = arenaA[:].bitcast(BF16)
    t1_fm = arAb[:, 0:8192]
    lam = arenaA[:, 0:1024]
    kfm = arenaA[:, 1024:2048]
    pfm = arenaA[:, 2048:3072]
    base_b = sb("l1a_baseb", [128, 1024], F32)[:]
    ow_s = arAb[:, 0:8192]
    Wcols = sb("l1a_wcols", [128, 16 * 32], F32)
    atm = sb("l1a_atm", [128, 16 * 128], BF16)
    kT = sb("l1a_kT", [128, 16 * 64], BF16)
    S_all = sb("l1a_S", [128, 512], F32)
    S_bf = sb("l1a_Sbf", [128, 512], BF16)
    xob = xch
    qw_s = [sb(f"l1a_qw{i}", [128, 8 * 128], BF16) for i in range(2)]
    kw_s = [sb(f"l1a_kw{i}", [128, 8 * 128], BF16) for i in range(2)]
    vw_s = sb("l1a_vw", [128, 8 * 512], BF16)
    csw1_s = sb("l1a_csw1", [128, 8 * 256], BF16)
    csw2_s = sb("l1a_csw2", [128, 2 * 16], BF16)
    csb1_s = sb("l1a_csb1", [128, 2], F32)
    csb2_r = sb("l1a_csb2", [1, 16], F32)
    dp_r = sb("l1a_dpr", [1, 1024], F32)

    x_t = x_ap.rearrange("(i p) d -> i p d", p=128)
    xo_t = xout_ap.rearrange("(i p) d -> i p d", p=128)
    bank_ev = {i: None for i in range(8)}

    def mm_waits(bank_idx, extra):
        w = list(extra)
        if bank_ev[bank_idx] is not None:
            w.append(bank_ev[bank_idx])
        return [x for x in w if x is not None]

    m_vw = {}
    m_cs1 = S.emit("sync", lambda e: e.dma_start(
        out=csw1_s[:].rearrange("p (cd i) -> p cd i", cd=8),
        in_=W["csw1T"].rearrange("(cd p) i -> p cd i", p=128)), dma=True)
    m_cs2 = S.emit("sync", lambda e: e.dma_start(
        out=csw2_s[:].rearrange("p (ci i) -> p ci i", ci=2),
        in_=W["csw2T"].rearrange("(ci p) i -> p ci i", p=128)), dma=True)
    m_csb1 = S.emit("sync", lambda e: e.dma_start(out=csb1_s[:], in_=W["csb1"][:, :]), dma=True)
    m_csb2 = S.emit("sync", lambda e: e.dma_start(out=csb2_r[:], in_=W["csb2"][:, :]), dma=True)
    m_dp = S.emit("sync", lambda e: e.dma_start(out=dp_r[:], in_=W["dparams"][:, :]), dma=True)
    m_sig = S.emit("scalar", lambda e: e.activation(dp_r[:], dp_r[:], AF.Sigmoid),
                   waits=[m_dp], inc=1)
    m_bb_mm = S.emit("tensor", lambda e: e.matmul(
        psum[7][:, 0:512], c["ones1f"][:], dp_r[0:1, 0:512], start=True, stop=True),
        waits=mm_waits(7, [m_sig] + list(consts_ms)), inc=1)
    m_bb_mm2 = S.emit("tensor", lambda e: e.matmul(
        psum[6][:, 0:512], c["ones1f"][:], dp_r[0:1, 512:1024], start=True, stop=True),
        waits=mm_waits(6, [m_sig]), inc=1)
    m_bb = S.emit("vector", lambda e: e.tensor_copy(base_b[:, 0:512], psum[7][:, 0:512]),
                  waits=[m_bb_mm], inc=1)
    m_bb2 = S.emit("vector", lambda e: e.tensor_copy(base_b[:, 512:1024], psum[6][:, 0:512]),
                   waits=[m_bb_mm2], inc=1)
    bank_ev[7] = m_bb
    bank_ev[6] = m_bb2
    m_s0 = S.emit("gpsimd", lambda e: e.memset(S_all[:], 0.0), inc=1)
    m_s0b = S.emit("gpsimd", lambda e: e.memset(S_bf[:], 0.0), inc=1)

    cons_m5 = {}
    tp_last = {}
    tp_ev = []
    ch_ready = {}
    stage = {}

    def ln_stats(i):
        w = []
        if i - 2 in cons_m5:
            w += cons_m5[i - 2]
        mld = S.emit("sync", lambda e, i=i: e.dma_start(out=xch[i % 2][:], in_=x_t[i]),
                     waits=w, dma=True)
        slc = lambda name, i=i: stats[name][:, i:i + 1]
        m1 = S.emit("scalar", lambda e, i=i, slc=slc: e.activation(
            xsq[:], xch[i % 2][:], AF.Square, accum_out=slc("sumsq")), waits=[mld], inc=1)
        mrd = S.emit("vector", lambda e, i=i, slc=slc: e.reduce_sum(
            slc("s").unsqueeze(2), xch[i % 2][:].rearrange("p (a b) -> p a b", a=1), axis=AX.X),
            waits=[mld], inc=1)
        mmm = S.emit("vector", lambda e, slc=slc: e.tensor_scalar_mul(slc("m"), slc("s"), 1.0 / 1024.0),
               self_waits=[mrd], inc=1)
        mv1 = S.emit("vector", lambda e, slc=slc: e.tensor_scalar_mul(slc("v1"), slc("sumsq"), 1.0 / 1024.0),
               waits=[m1], inc=1)
        m3 = S.emit("vector", lambda e, slc=slc: e.scalar_tensor_tensor(
            slc("mvar"), slc("m"), slc("m"), slc("v1"), OP.mult, OP.subtract),
            self_waits=[mmm, mv1], inc=1)
        mlnv = S.emit("scalar", lambda e, slc=slc: e.activation(slc("lnv"), slc("mvar"), AF.Ln,
                                                         bias=c["epsb"][:], scale=-1.0),
               waits=[m3], inc=1)
        m4 = S.emit("scalar", lambda e, slc=slc: e.activation(slc("rstd"), slc("lnv"), AF.Exp,
                                                              scale=-0.5), self_waits=[mlnv], inc=1)
        stage[i] = (slc, m1, m4)

    def ln_apply(i):
        slc, m1, m4 = stage[i]
        after = [tp_last[i - 1]] if i - 1 in tp_last else []
        mnm = S.emit("vector", lambda e, slc=slc: e.tensor_mul(slc("nm"), slc("m"), slc("rstd")),
               waits=[m4], inc=1)
        m5 = S.emit("vector", lambda e, i=i, slc=slc: e.tensor_scalar(
            xlns[i % 2][:], xch[i % 2][:], slc("rstd"), slc("nm"), OP.mult, OP.subtract),
            waits=after, self_waits=[mnm], inc=1)
        cons_m5[i] = [m5, m1]
        for grp in range(2):
            bidx = 2 + grp
            pbt = psum[bidx].bitcast(BF16)
            mts = []
            for k in range(4):
                cdx = grp * 4 + k
                tw = [m5] + ([bank_ev[bidx]] if bank_ev[bidx] else [])
                mt = S.emit("tensor", lambda e, s=pbt[:, k * 128:(k + 1) * 128], i=i, cdx=cdx: e.transpose(
                    s, xlns[i % 2][:, cdx * 128:(cdx + 1) * 128], c["identb"][:]),
                    waits=tw if k == 0 else [], inc=(1 if k == 3 else 0))
                mts.append(mt)
            for k in range(4):
                cdx = grp * 4 + k
                dst = xln_fm[:, cdx * 2048 + i * 128: cdx * 2048 + (i + 1) * 128]
                mev = S.emit("scalar", lambda e, s=pbt[:, k * 128:(k + 1) * 128], d=dst: e.copy(d, s),
                             waits=[mts[-1]], inc=1)
                tp_ev.append(mev)
            bank_ev[bidx] = tp_ev[-1]
            tp_last[i] = mts[-1]
        ch_ready[i] = tp_ev[-1]

    for i in range(TCH + 2):
        if i - 2 >= 0:
            ln_apply(i - 2)
        if i < TCH:
            ln_stats(i)
    xln_ready = tp_ev[-1]
    tb_ready = {tb: ch_ready[4 * tb + 3] for tb in range(4)}

    t1_ev = []
    for ich in range(2):
        for tb in range(4):
            bidx = 6 + ((ich * 4 + tb) % 2)
            extra = [m_cs1, m_csb1, tb_ready[tb]]
            for cdx in range(DCH):
                m = S.emit("tensor", lambda e, bidx=bidx, ich=ich, tb=tb, cdx=cdx: e.matmul(
                    psum[bidx][:, :], csw1_s[:, cdx * 256 + ich * 128: cdx * 256 + (ich + 1) * 128],
                    xln_fm[:, cdx * 2048 + tb * 512: cdx * 2048 + (tb + 1) * 512],
                    start=(cdx == 0), stop=(cdx == DCH - 1)),
                    waits=mm_waits(bidx, extra) if cdx == 0 else [],
                    inc=(1 if cdx == DCH - 1 else 0))
            mev = S.emit("scalar", lambda e, bidx=bidx, ich=ich, tb=tb: e.activation(
                t1_fm[:, ich * 2048 + tb * 512: ich * 2048 + (tb + 1) * 512], psum[bidx][:, :],
                AF.Tanh, bias=csb1_s[:, ich:ich + 1]), waits=[m], inc=1)
            bank_ev[bidx] = mev
            t1_ev.append(mev)
    lcs_ev = []
    for i in range(TCH):
        bidx = 4 + (i % 2)
        extra = [t1_ev[-1], m_cs2, m_csb2]
        for ich in range(2):
            m = S.emit("tensor", lambda e, bidx=bidx, i=i, ich=ich: e.matmul(
                psum[bidx][:, 0:16], t1_fm[:, ich * 2048 + i * 128: ich * 2048 + (i + 1) * 128],
                csw2_s[:, ich * 16:(ich + 1) * 16],
                start=(ich == 0), stop=False),
                waits=mm_waits(bidx, extra) if ich == 0 else [], inc=0)
        m = S.emit("tensor", lambda e, bidx=bidx, i=i: e.matmul(
            psum[bidx][0:128, 0:16], c["ones1f"][:], csb2_r[:],
            start=False, stop=True), inc=1)
        msg = S.emit("scalar", lambda e, bidx=bidx, i=i: e.activation(
            xsq[:, (i % 2) * 16:(i % 2) * 16 + 16], psum[bidx][:, 0:16], AF.Sigmoid),
            waits=[m], inc=1)
        mev = S.emit("vector", lambda e, i=i: e.tensor_scalar(
            lcs_tm[:, i * 16:(i + 1) * 16], xsq[:, (i % 2) * 16:(i % 2) * 16 + 16],
            0.2, 1.0, OP.mult, OP.add), waits=[msg], inc=1)
        bank_ev[bidx] = mev
        lcs_ev.append(mev)
    lcs_all = lcs_ev[-1]

    qk_ev = {}
    dms = {}
    for which, wsrc, w_s, dstb in (("q", W["qw1T"], qw_s, q_fm), ("k", W["kw1T"], kw_s, k_fm)):
        for fch in range(8):
            p = fch % 2
            wdep = [qk_ev[(which, fch - 2, 3)]] if (which, fch - 2, 3) in qk_ev else []
            dms[(which, fch)] = S.emit("sync", lambda e, wsrc=wsrc, w_s=w_s, p=p, fch=fch: e.dma_start(
                out=w_s[p][:].rearrange("p (cd i) -> p cd i", cd=8),
                in_=wsrc.rearrange("(cd p) i -> p cd i", p=128)[:, :, fch * 128:(fch + 1) * 128]),
                waits=wdep, dma=True)
            for tb in range(4):
                bidx = tb % 2
                extra = [dms[(which, fch)], tb_ready[tb]]
                for cdx in range(DCH):
                    m = S.emit("tensor", lambda e, bidx=bidx, fch=fch, tb=tb, cdx=cdx, p=p, w_s=w_s: e.matmul(
                        psum[bidx][:, :], w_s[p][:, cdx * 128:(cdx + 1) * 128],
                        xln_fm[:, cdx * 2048 + tb * 512: cdx * 2048 + (tb + 1) * 512],
                        start=(cdx == 0), stop=(cdx == DCH - 1)),
                        waits=mm_waits(bidx, extra) if cdx == 0 else [],
                        inc=(1 if cdx == DCH - 1 else 0))
                mev = S.emit("scalar", lambda e, bidx=bidx, fch=fch, tb=tb, dstb=dstb: e.copy(
                    dstb[:, fch * 2048 + tb * 512: fch * 2048 + (tb + 1) * 512], psum[bidx][:, :]),
                    waits=[m], inc=1)
                bank_ev[bidx] = mev
                qk_ev[(which, fch, tb)] = mev
    v_ev = {}
    last_vmm = None
    for ib in range(2):
        vdep = [last_vmm] if last_vmm else []
        m_vw[ib] = S.emit("sync", lambda e, ib=ib: e.dma_start(
            out=vw_s[:].rearrange("p (cd i) -> p cd i", cd=8),
            in_=W["vw1T"].rearrange("(cd p) i -> p cd i", p=128)[:, :, ib * 512:(ib + 1) * 512]),
            waits=vdep, dma=True)
        for i in range(TCH):
            bidx = 2 + (i % 2)
            extra = [m_vw[ib], ch_ready[i]]
            for cdx in range(DCH):
                m = S.emit("tensor", lambda e, bidx=bidx, i=i, ib=ib, cdx=cdx: e.matmul(
                    psum[bidx][:, :], xln_fm[:, cdx * 2048 + i * 128: cdx * 2048 + (i + 1) * 128],
                    vw_s[:, cdx * 512:(cdx + 1) * 512],
                    start=(cdx == 0), stop=(cdx == DCH - 1)),
                    waits=mm_waits(bidx, extra) if cdx == 0 else [],
                    inc=(1 if cdx == DCH - 1 else 0))
            mev = S.emit("scalar", lambda e, bidx=bidx, i=i, ib=ib: e.copy(
                v_tm[:, i * 1024 + ib * 512: i * 1024 + (ib + 1) * 512], psum[bidx][:, :]),
                waits=[m], inc=1)
            bank_ev[bidx] = mev
            v_ev[(i, ib)] = mev
        last_vmm = m

    fac_ms = {}
    qk_scaled = {}
    prev_fac = {}
    for i in range(TCH):
        lam_w = [lcs_all, m_bb, m_bb2]
        if i - 1 in fac_ms:
            lam_w.append(fac_ms[i - 1])
        mlam0 = S.emit("vector", lambda e, i=i: e.tensor_tensor(
            lam.rearrange("p (h f) -> p h f", f=64),
            base_b.rearrange("p (h f) -> p h f", f=64),
            lcs_tm[:, i * 16:(i + 1) * 16].unsqueeze(2).broadcast_to([128, 16, 64]),
            OP.mult), waits=lam_w, inc=1)
        mlam = S.emit("vector", lambda e: e.tensor_scalar_min(lam[:, :], lam[:, :], 0.9995),
                      self_waits=[mlam0], inc=1)
        mln = S.emit("scalar", lambda e: e.activation(lam[:, :], lam[:, :], AF.Ln),
                     waits=[mlam], inc=1)
        mp32 = None
        for fch in range(8):
            bidx = 4 + (fch % 2)
            m = S.emit("tensor", lambda e, bidx=bidx, fch=fch: e.matmul(
                psum[bidx][:, 0:128], lam[:, fch * 128:(fch + 1) * 128], c["lotri_blk32"][:],
                start=True, stop=True),
                waits=mm_waits(bidx, [mln] + list(consts_ms)), inc=1)
            mp32 = S.emit("scalar", lambda e, bidx=bidx, fch=fch: e.activation(
                pfm[:, fch * 128:(fch + 1) * 128], psum[bidx][:, 0:128], AF.Exp),
                waits=[m], self_waits=[mp32] if mp32 else [], inc=1)
            bank_ev[bidx] = mp32
        mcl = S.emit("vector", lambda e: e.reciprocal(kfm[:, :], pfm[:, :]),
                     waits=[mp32], inc=1)
        mkx = S.emit("vector", lambda e: e.tensor_scalar_min(kfm[:, :], kfm[:, :], 3.733e32),
                     self_waits=[mcl], inc=1)
        mq = S.emit("vector", lambda e, i=i: e.tensor_tensor(
            q_fm[:].rearrange("p (f t) -> p f t", t=2048)[:, :, i * 128:(i + 1) * 128],
            q_fm[:].rearrange("p (f t) -> p f t", t=2048)[:, :, i * 128:(i + 1) * 128],
            pfm.rearrange("p (f t) -> p f t", t=128), OP.mult),
            waits=[mp32] + [qk_ev[("q", fch, i // 4)] for fch in range(8)], inc=1)
        mk = S.emit("vector", lambda e, i=i: e.tensor_tensor(
            k_fm[:].rearrange("p (f t) -> p f t", t=2048)[:, :, i * 128:(i + 1) * 128],
            k_fm[:].rearrange("p (f t) -> p f t", t=2048)[:, :, i * 128:(i + 1) * 128],
            kfm.rearrange("p (f t) -> p f t", t=128), OP.mult),
            waits=[qk_ev[("k", fch, i // 4)] for fch in range(8)], self_waits=[mkx], inc=1)
        for fch in range(8):
            qk_scaled[(i, fch)] = (mq, mk)
        mwc = S.emit("vector", lambda e, i=i: e.tensor_copy(
            Wcols[:, i * 32:(i + 1) * 32].rearrange("p (f b) -> p f b", b=4).unsqueeze(3),
            pfm.rearrange("p (f b q) -> p f b q", b=4, q=32)[:, :, :, 31:32]),
            waits=[mp32], inc=1)
        fac_ms[i] = mwc

    m_ow = S.emit("sync", lambda e: e.dma_start(
        out=ow_s.rearrange("p (ci d) -> p ci d", ci=8),
        in_=W["ow1T"].rearrange("(ci p) d -> p ci d", p=128)),
        waits=[fac_ms[TCH - 1]], dma=True)
    sbf_ms = {0: m_s0b, 1: m_s0b}
    sall_ms = {0: m_s0, 1: m_s0}
    at_ev = {}
    kt_ev = {}
    out_ev = {}
    oproj_ms = {}
    fin = []
    def atcol(h):
        return ((h % 2) * 2 + (h // 8)) * 512 + ((h // 2) % 4) * 128

    for i in range(L1A_MAXCH):
        grp_last = {}
        for h in range(16):
            fch = h // 2
            rows = slice((h % 2) * 64, (h % 2) * 64 + 64)
            bidx = h % 2
            slot = (h // 2) % 4
            extra = [qk_scaled[(i, fch)][0], qk_scaled[(i, fch)][1]]
            if i - 1 in oproj_ms:
                extra.append(oproj_ms[i - 1])
            m = S.emit("tensor", lambda e, bidx=bidx, slot=slot, i=i, fch=fch, rows=rows: e.matmul(
                psum[bidx][:, slot * 128:(slot + 1) * 128],
                k_fm[rows, fch * 2048 + i * 128: fch * 2048 + (i + 1) * 128],
                q_fm[rows, fch * 2048 + i * 128: fch * 2048 + (i + 1) * 128],
                start=True, stop=True),
                waits=mm_waits(bidx, extra), inc=1)
            grp_last[bidx] = m
            if h in (7, 15):
                g8 = h // 8
                for b2 in range(2):
                    off = (b2 * 2 + g8) * 512
                    mev = S.emit("vector", lambda e, b2=b2, off=off: e.tensor_tensor(
                        atm[:, off:off + 512].rearrange("p (s c) -> p s c", c=128),
                        psum[b2].rearrange("p (s c) -> p s c", c=128),
                        c["mask_bd32"][:].unsqueeze(1).broadcast_to([128, 4, 128]),
                        OP.mult), waits=[grp_last[b2]], inc=1)
                    bank_ev[b2] = mev
                    at_ev[(i, g8 * 2 + b2)] = mev
        at_all = [at_ev[(i, g)] for g in range(4)]
        for grp in range(2 if L1A_SUB >= 2 else 0):
            bidx = 2 + (grp % 2)
            pbt = psum[bidx].bitcast(BF16)
            mts = []
            for k2 in range(4):
                fch = grp * 4 + k2
                tw = ([qk_scaled[(i, f2)][1] for f2 in range(8)]
                      + ([bank_ev[bidx]] if bank_ev[bidx] else []))
                mt = S.emit("tensor", lambda e, s=pbt[0:128, 512 + k2 * 128: 512 + (k2 + 1) * 128],
                            i=i, fch=fch: e.transpose(
                    s, k_fm[:, fch * 2048 + i * 128: fch * 2048 + (i + 1) * 128],
                    c["identb"][:]),
                    waits=tw if k2 == 0 else [], inc=(1 if k2 == 3 else 0))
                mts.append(mt)
            for k2 in range(4):
                fch = grp * 4 + k2
                mev = S.emit("scalar", lambda e, s=pbt[0:128, 512 + k2 * 128: 512 + (k2 + 1) * 128],
                             fch=fch: e.copy(
                    kT[:, fch * 128:(fch + 1) * 128], s), waits=[mts[-1]], inc=1)
            bank_ev[bidx] = mev
            kt_ev[(i, grp)] = mev
        kt_all = mev if L1A_SUB >= 2 else None
        first_out_mm = {}
        for h in range(16 if L1A_SUB >= 3 else 0):
            a = h // 2
            bidx = 2 + (a // 4)
            rows = slice((h % 2) * 64, (h % 2) * 64 + 64)
            extra = at_all + [v_ev[(i, 1)], v_ev[(i, 0)]]
            if i - 1 in out_ev:
                extra.append(out_ev[i - 1])
            m = S.emit("tensor", lambda e, bidx=bidx, i=i, h=h, a=a, rows=rows: e.matmul(
                psum[bidx][rows, (a % 4) * 128:(a % 4 + 1) * 128],
                v_tm[:, i * 1024 + h * 64: i * 1024 + (h + 1) * 64],
                atm[:, atcol(h): atcol(h) + 128],
                start=(h in (0, 8)), stop=False),
                waits=mm_waits(bidx, extra) if h in (0, 8) else [], inc=0)
            first_out_mm[h] = m
        st_final = {0: None, 1: None}
        for b in range(4 if L1A_SUB >= 4 else 0):
            for grp2 in range(2):
                for h in [hh for hh in range(16) if (hh // 2) // 4 == grp2]:
                    a = h // 2
                    bidx = 2 + (a // 4)
                    rows = slice((h % 2) * 64, (h % 2) * 64 + 64)
                    m = S.emit("tensor", lambda e, bidx=bidx, i=i, h=h, a=a, b=b, rows=rows: e.matmul(
                        psum[bidx][rows, (a % 4) * 128 + b * 32: (a % 4) * 128 + (b + 1) * 32],
                        S_bf[rows, a * 64:(a + 1) * 64],
                        q_fm[rows, (h // 2) * 2048 + i * 128 + b * 32: (h // 2) * 2048 + i * 128 + (b + 1) * 32],
                        start=False, stop=(b == 3)),
                        waits=[sbf_ms[grp2]] if h in (0, 8) else [], inc=(1 if h in (7, 15) else 0))
                st_done_g = m
                st_final[grp2] = m
                kb = (6, 5)[b % 2] if grp2 == 0 else (4, 7)[b % 2]
                for a in range(grp2 * 4, grp2 * 4 + 4):
                    col = (a % 4) * 128
                    brows = slice(b * 32, (b + 1) * 32)
                    m = S.emit("tensor", lambda e, i=i, a=a, b=b, kb=kb, col=col, brows=brows: e.matmul(
                        psum[kb][:, col:col + 128],
                        kT[brows, a * 128:(a + 1) * 128],
                        v_tm[brows, i * 1024 + a * 128: i * 1024 + (a + 1) * 128],
                        start=(a % 4 == 0), stop=(a % 4 == 3), tile_position=(b * 32, 0)),
                        waits=mm_waits(kb, [kt_all]) if a % 4 == 0 else [],
                        inc=(1 if a % 4 == 3 else 0))
                kv_done_g = m
                cs = grp2 * 256
                m_s1a = S.emit("vector", lambda e, cs=cs, kb=kb: e.tensor_tensor(
                    S_all[0:64, cs:cs + 256].rearrange("p (a f) -> p a f", f=64),
                    S_all[0:64, cs:cs + 256].rearrange("p (a f) -> p a f", f=64),
                    psum[kb].rearrange("p (a f) -> p a f", f=128)[0:64, 0:4, 0:64], OP.add),
                    waits=[kv_done_g, sbf_ms[grp2], sall_ms[grp2]], inc=1)
                m_s1b = S.emit("vector", lambda e, cs=cs, kb=kb: e.tensor_tensor(
                    S_all[64:128, cs:cs + 256].rearrange("p (a f) -> p a f", f=64),
                    S_all[64:128, cs:cs + 256].rearrange("p (a f) -> p a f", f=64),
                    psum[kb].rearrange("p (a f) -> p a f", f=128)[64:128, 0:4, 64:128], OP.add), inc=1)
                m_s2 = S.emit("vector", lambda e, i=i, b=b, cs=cs, grp2=grp2: e.tensor_tensor(
                    S_all[:, cs:cs + 256].rearrange("p (f q) -> p f q", q=64),
                    S_all[:, cs:cs + 256].rearrange("p (f q) -> p f q", q=64),
                    Wcols[:].rearrange("p (t f b) -> p t f b", f=8, b=4)[:, i, grp2 * 4:grp2 * 4 + 4, b:b + 1].broadcast_to([128, 4, 64]),
                    OP.mult), inc=1)
                sall_ms[grp2] = m_s2
                m_s3 = S.emit("vector", lambda e, cs=cs: e.tensor_copy(
                    S_bf[:, cs:cs + 256], S_all[:, cs:cs + 256]),
                    waits=[st_done_g], inc=1)
                sbf_ms[grp2] = m_s3
                bank_ev[kb] = m_s2
        oevs = []
        for a in range(8 if L1A_SUB >= 5 else 0):
            bidx = 2 + (a // 4)
            mev = S.emit("scalar", lambda e, bidx=bidx, a=a, i=i: e.copy(
                y_fm[:, a * 2048 + i * 128: a * 2048 + (i + 1) * 128],
                psum[bidx][:, (a % 4) * 128:(a % 4 + 1) * 128]),
                waits=[x for x in [st_final[a // 4]] if x], inc=1)
            bank_ev[bidx] = mev
            oevs.append(mev)
        out_ev[i] = oevs[-1] if oevs else at_ev[(i, 3)]
        mld = S.emit("sync", lambda e, i=i: e.dma_start(out=xch[i % 2][:], in_=x_t[i]),
                     waits=[x for x in ([oproj_ms[i - 2]] if i >= 2 else []) + [cons_m5[15][0], cons_m5[15][1]] if x],
                     dma=True)
        evs2 = []
        for dh in range(2 if L1A_SUB >= 6 else 0):
            bidx = 2 + dh
            extra = oevs + [m_ow]
            for ych in range(8):
                m = S.emit("tensor", lambda e, bidx=bidx, i=i, dh=dh, ych=ych: e.matmul(
                    psum[bidx][:, :], y_fm[:, ych * 2048 + i * 128: ych * 2048 + (i + 1) * 128],
                    ow_s[:, ych * 1024 + dh * 512: ych * 1024 + (dh + 1) * 512],
                    start=(ych == 0), stop=(ych == 7)),
                    waits=mm_waits(bidx, extra) if ych == 0 else [],
                    inc=(1 if ych == 7 else 0))
            mev = S.emit("vector", lambda e, bidx=bidx, i=i, dh=dh: e.tensor_tensor(
                xob[i % 2][:, dh * 512:(dh + 1) * 512], xch[i % 2][:, dh * 512:(dh + 1) * 512],
                psum[bidx][:, :], OP.add),
                waits=[m, mld] + ([oproj_ms[i - 2]] if i - 2 in oproj_ms else []), inc=1)
            bank_ev[bidx] = mev
            evs2.append(mev)
        if L1A_SUB >= 6:
            mst = S.emit("gpsimd", lambda e, i=i: e.dma_start(out=xo_t[i], in_=xob[i % 2][:]),
                         waits=evs2, dma=True)
            oproj_ms[i] = mst
            fin.append(mst)
    dbg = dict(q_fm=q_fm, k_fm=k_fm, v_tm=v_tm, lcs_tm=lcs_tm, Wcols=Wcols,
               atm=atm, kT=kT, S_all=S_all, y_fm=y_fm, xln_fm=xln_fm, base_b=base_b)
    return fin, dbg



def emit_l1_ffn(nc, S, es, c, psum, x_ap, xout_ap, W, consts_ms):
    sb = lambda n, s, d: es.enter_context(nc.sbuf_tensor(n, s, d))
    xch = [sb(f"l1f_xch{i}", [128, 1024], F32) for i in range(2)]
    xsq = sb("l1f_xsq", [128, 1024], F32)
    stats = alloc_stats(nc, es, "l1f")
    xlns = [sb("l1f_xlns0", [128, 1024], BF16)] * 2
    xln_fm = sb("l1f_xlnfm", [128, 8 * 2048], BF16)
    a_fm = sb("l1f_afm", [128, 8 * 2048], BF16)
    delta = sb("l1f_delta", [128, 16 * 1024], BF16)
    glog = sb("l1f_glog", [128, 16 * 8], F32)
    cw = sb("l1f_cw", [128, 16 * 8], F32)
    gstat = sb("l1f_gstat", [128, 16], F32)
    ind1 = sb("l1f_ind1", [128, 16 * 8], F32)
    gmsk = sb("l1f_gmsk", [128, 8], F32)
    gateb_b = sb("l1f_gatebb", [128, 8], F32)
    sb2_b = sb("l1f_sb2b", [128, 1024], F32)
    gb_r = sb("l1f_gbr", [1, 8], F32)
    sb2_r = sb("l1f_sb2r", [1, 1024], F32)
    b1_s = [sb(f"l1f_b1{i}", [128, 16], F32) for i in range(2)]
    b3_s = [sb(f"l1f_b3{i}", [128, 16], F32) for i in range(2)]
    w13_s = [sb(f"l1f_w13{i}", [128, 4 * 1024], BF16) for i in range(2)]
    w2_s = [sb(f"l1f_w2{i}", [128, 8 * 1024], BF16) for i in range(2)]
    scr_e1 = [sb(f"l1f_e1{i}", [128, 512], F32) for i in range(2)]
    scr_hg = [sb(f"l1f_hg{i}", [128, 512], F32) for i in range(2)]
    scr_sg = [sb(f"l1f_sg{i}", [128, 512], F32) for i in range(2)]
    scr_l1 = [sb(f"l1f_l1{i}", [128, 512], F32) for i in range(2)]
    scr_hl = [sb(f"l1f_hl{i}", [128, 512], F32) for i in range(2)]

    x_t = x_ap.rearrange("(i p) d -> i p d", p=128)
    xo_t = xout_ap.rearrange("(i p) d -> i p d", p=128)
    bank_ev = {i: None for i in range(8)}

    def mm_waits(bank_idx, extra):
        w = list(extra)
        if bank_ev[bank_idx] is not None:
            w.append(bank_ev[bank_idx])
        return [x for x in w if x is not None]

    m_gw = S.emit("sync", lambda e: e.dma_start(out=gb_r[:], in_=W["gateb"][:, :]), dma=True)
    m_sb2 = S.emit("sync", lambda e: e.dma_start(out=sb2_r[:], in_=W["sb2"][:, :]), dma=True)
    gatew_s = sb("l1f_gatew", [128, 8 * 8], BF16)
    m_gws = S.emit("sync", lambda e: e.dma_start(
        out=gatew_s[:].rearrange("p (cd i) -> p cd i", cd=8),
        in_=W["gatewT"].rearrange("(cd p) i -> p cd i", p=128)), dma=True)
    m_gb_mm = S.emit("tensor", lambda e: e.matmul(
        psum[4][:, 0:8], c["ones1f"][:], gb_r[:], start=True, stop=True),
        waits=mm_waits(4, [m_gw] + list(consts_ms)), inc=1)
    m_gbb = S.emit("vector", lambda e: e.tensor_copy(gateb_b[:], psum[4][:, 0:8]),
                   waits=[m_gb_mm], inc=1)
    bank_ev[4] = m_gbb
    m_sb2_mm = S.emit("tensor", lambda e: e.matmul(
        psum[5][:, :], c["ones1f"][:], sb2_r[0:1, 0:512], start=True, stop=True),
        waits=mm_waits(5, [m_sb2]), inc=1)
    m_sb2b = S.emit("vector", lambda e: e.tensor_copy(sb2_b[:, 0:512], psum[5][:, :]),
                    waits=[m_sb2_mm], inc=1)
    bank_ev[5] = m_sb2b
    m_sb2_mm2 = S.emit("tensor", lambda e: e.matmul(
        psum[6][:, :], c["ones1f"][:], sb2_r[0:1, 512:1024], start=True, stop=True),
        waits=mm_waits(6, [m_sb2]), inc=1)
    m_sb2b2 = S.emit("vector", lambda e: e.tensor_copy(sb2_b[:, 512:1024], psum[6][:, :]),
                     waits=[m_sb2_mm2], inc=1)
    bank_ev[6] = m_sb2b2

    cons_m5 = {}
    tp_last = {}
    tp_ev = []
    ch_ready = {}
    cw_ms = {}
    stage = {}

    def ln_stats(i):
        w = []
        if i - 2 in cons_m5:
            w += cons_m5[i - 2]
        mld = S.emit("sync", lambda e, i=i: e.dma_start(out=xch[i % 2][:], in_=x_t[i]),
                     waits=w, dma=True)
        slc = lambda name, i=i: stats[name][:, i:i + 1]
        m1 = S.emit("scalar", lambda e, i=i, slc=slc: e.activation(
            xsq[:], xch[i % 2][:], AF.Square, accum_out=slc("sumsq")), waits=[mld], inc=1)
        mrd = S.emit("vector", lambda e, i=i, slc=slc: e.reduce_sum(
            slc("s").unsqueeze(2), xch[i % 2][:].rearrange("p (a b) -> p a b", a=1), axis=AX.X),
            waits=[mld], inc=1)
        mmm = S.emit("vector", lambda e, slc=slc: e.tensor_scalar_mul(slc("m"), slc("s"), 1.0 / 1024.0),
               self_waits=[mrd], inc=1)
        mv1 = S.emit("vector", lambda e, slc=slc: e.tensor_scalar_mul(slc("v1"), slc("sumsq"), 1.0 / 1024.0),
               waits=[m1], inc=1)
        m3 = S.emit("vector", lambda e, slc=slc: e.scalar_tensor_tensor(
            slc("mvar"), slc("m"), slc("m"), slc("v1"), OP.mult, OP.subtract),
            self_waits=[mmm, mv1], inc=1)
        mlnv = S.emit("scalar", lambda e, slc=slc: e.activation(slc("lnv"), slc("mvar"), AF.Ln,
                                                         bias=c["epsb"][:], scale=-1.0),
               waits=[m3], inc=1)
        m4 = S.emit("scalar", lambda e, slc=slc: e.activation(slc("rstd"), slc("lnv"), AF.Exp,
                                                              scale=-0.5), self_waits=[mlnv], inc=1)
        stage[i] = (slc, m1, m4)

    def ln_apply(i):
        slc, m1, m4 = stage[i]
        after = [tp_last[i - 1]] if i - 1 in tp_last else []
        mnm = S.emit("vector", lambda e, slc=slc: e.tensor_mul(slc("nm"), slc("m"), slc("rstd")),
               waits=[m4], inc=1)
        m5 = S.emit("vector", lambda e, i=i, slc=slc: e.tensor_scalar(
            xlns[i % 2][:], xch[i % 2][:], slc("rstd"), slc("nm"), OP.mult, OP.subtract),
            waits=after, self_waits=[mnm], inc=1)
        cons_m5[i] = [m5, m1]
        for grp in range(2):
            bidx = 2 + grp
            pbt = psum[bidx].bitcast(BF16)
            mts = []
            for k in range(4):
                cdx = grp * 4 + k
                tw = [m5] + ([bank_ev[bidx]] if bank_ev[bidx] else [])
                mt = S.emit("tensor", lambda e, s=pbt[:, k * 128:(k + 1) * 128], i=i, cdx=cdx: e.transpose(
                    s, xlns[i % 2][:, cdx * 128:(cdx + 1) * 128], c["identb"][:]),
                    waits=tw if k == 0 else [], inc=(1 if k == 3 else 0))
                mts.append(mt)
            for k in range(4):
                cdx = grp * 4 + k
                dst = xln_fm[:, cdx * 2048 + i * 128: cdx * 2048 + (i + 1) * 128]
                mev = S.emit("scalar", lambda e, s=pbt[:, k * 128:(k + 1) * 128], d=dst: e.copy(d, s),
                             waits=[mts[-1]], inc=1)
                tp_ev.append(mev)
            bank_ev[bidx] = tp_ev[-1]
            tp_last[i] = mts[-1]
        ch_ready[i] = tp_ev[-1]
        fm_ready_i = tp_ev[-1]
        extra = [fm_ready_i, m_gws]
        for cdx in range(DCH):
            m = S.emit("tensor", lambda e, i=i, cdx=cdx: e.matmul(
                psum[4][:, 0:8], xln_fm[:, cdx * 2048 + i * 128: cdx * 2048 + (i + 1) * 128],
                gatew_s[:, cdx * 8:(cdx + 1) * 8],
                start=(cdx == 0), stop=(cdx == DCH - 1)),
                waits=mm_waits(4, extra) if cdx == 0 else [],
                inc=(1 if cdx == DCH - 1 else 0))
        mgl = S.emit("vector", lambda e, i=i: e.tensor_copy(glog[:, i * 8:(i + 1) * 8],
                                                            psum[4][:, 0:8]), waits=[m], inc=1)
        bank_ev[4] = mgl
        def mxt(buf_off, out_col, prev, name_i=i):
            ma_ = S.emit("vector", lambda e, buf_off=buf_off: e.tensor_tensor(
                gmsk[:, 0:4], cw[:, buf_off:buf_off + 4], cw[:, buf_off + 4:buf_off + 8], OP.max),
                self_waits=[prev], inc=1)
            mb_ = S.emit("vector", lambda e: e.tensor_tensor(
                gmsk[:, 0:2], gmsk[:, 0:2], gmsk[:, 2:4], OP.max), self_waits=[ma_], inc=1)
            mc_ = S.emit("vector", lambda e, out_col=out_col: e.tensor_tensor(
                gstat[:, out_col:out_col + 1], gmsk[:, 0:1], gmsk[:, 1:2], OP.max),
                self_waits=[mb_], inc=1)
            return mc_
        gcol = i % 2
        mgl2 = S.emit("vector", lambda e, i=i: e.tensor_copy(
            cw[:, i * 8:(i + 1) * 8], glog[:, i * 8:(i + 1) * 8]), self_waits=[mgl], inc=1)
        mgm = mxt(i * 8, gcol, mgl2)
        mneg = S.emit("vector", lambda e, gcol=gcol: e.tensor_scalar_mul(
            gstat[:, gcol:gcol + 1], gstat[:, gcol:gcol + 1], -1.0), self_waits=[mgm], inc=1)
        msc = S.emit("scalar", lambda e, i=i, gcol=gcol: e.activation(
            glog[:, i * 8:(i + 1) * 8], glog[:, i * 8:(i + 1) * 8], AF.Exp,
            bias=gstat[:, gcol:gcol + 1], accum_out=cw[:, i * 8:i * 8 + 1]),
            waits=[mneg], inc=1)
        mrcp = S.emit("vector", lambda e, i=i: e.reciprocal(
            cw[:, i * 8:i * 8 + 1], cw[:, i * 8:i * 8 + 1]), waits=[msc], inc=1)
        msn = S.emit("vector", lambda e, i=i: e.tensor_scalar_mul(
            glog[:, i * 8:(i + 1) * 8], glog[:, i * 8:(i + 1) * 8], cw[:, i * 8:i * 8 + 1]),
            self_waits=[mrcp], inc=1)
        mb = S.emit("vector", lambda e, i=i: e.tensor_tensor(
            cw[:, i * 8:(i + 1) * 8], glog[:, i * 8:(i + 1) * 8], gateb_b[:], OP.add),
            waits=[m_gbb], self_waits=[msn], inc=1)
        mm1 = mxt(i * 8, gcol, mb)
        mi1 = S.emit("vector", lambda e, i=i, gcol=gcol: e.tensor_tensor(
            ind1[:, i * 8:(i + 1) * 8].unsqueeze(2), cw[:, i * 8:(i + 1) * 8].unsqueeze(2),
            gstat[:, gcol:gcol + 1].unsqueeze(2).broadcast_to([128, 8, 1]), OP.is_ge),
            self_waits=[mm1], inc=1)
        mgk = S.emit("vector", lambda e, i=i: e.tensor_scalar_mul(
            gmsk[:], ind1[:, i * 8:(i + 1) * 8], -1e30), self_waits=[mi1], inc=1)
        mmsk = S.emit("vector", lambda e, i=i: e.tensor_tensor(
            cw[:, i * 8:(i + 1) * 8], cw[:, i * 8:(i + 1) * 8], gmsk[:, 0:8], OP.add),
            self_waits=[mgk], inc=1)
        mm2 = mxt(i * 8, gcol, mmsk)
        mi2 = S.emit("vector", lambda e, i=i, gcol=gcol: e.tensor_tensor(
            cw[:, i * 8:(i + 1) * 8].unsqueeze(2), cw[:, i * 8:(i + 1) * 8].unsqueeze(2),
            gstat[:, gcol:gcol + 1].unsqueeze(2).broadcast_to([128, 8, 1]), OP.is_ge),
            self_waits=[mm2], inc=1)
        mi12 = S.emit("vector", lambda e, i=i: e.tensor_tensor(
            cw[:, i * 8:(i + 1) * 8], cw[:, i * 8:(i + 1) * 8], ind1[:, i * 8:(i + 1) * 8], OP.add),
            self_waits=[mi2], inc=1)
        mcw = S.emit("vector", lambda e, i=i: e.tensor_tensor(
            cw[:, i * 8:(i + 1) * 8], cw[:, i * 8:(i + 1) * 8], glog[:, i * 8:(i + 1) * 8], OP.mult),
            self_waits=[mi12], inc=1)
        mcw2 = S.emit("vector", lambda e, i=i: e.tensor_scalar_mul(
            cw[:, i * 8:(i + 1) * 8], cw[:, i * 8:(i + 1) * 8], 1.0 / 1.702),
            self_waits=[mcw], inc=1)
        cw_ms[i] = mcw2

    for i in range(TCH + 2):
        if i - 2 >= 0:
            ln_apply(i - 2)
        if i < TCH:
            ln_stats(i)
    fm_all = tp_ev[-1]
    tb_ready = {tb: ch_ready[4 * tb + 3] for tb in range(4)}
    cw_all = cw_ms[TCH - 1]
    delta_ms = {}
    y_ev = {}
    fin = []
    last_y_done = None
    w13_dma = {}
    w2_dma = {}
    b_dma = {}
    prev_scr = {}
    a_ms = {}
    for e8 in range(9):
        is_sh = (e8 == 8)
        p = e8 % 2
        wdep = [w13_consumed[e8 - 2]] if False else []
        wdep = [a_ms.get(e8 - 2)]
        bp = e8 % 2
        if is_sh:
            b_dma[e8] = S.emit("sync", lambda e, bp=bp: e.dma_start(
                out=b1_s[bp][:], in_=W["sb1p"][:, :]),
                waits=[x for x in wdep if x], dma=True)
            b_dma[(e8, 3)] = S.emit("sync", lambda e, bp=bp: e.dma_start(
                out=b3_s[bp][:], in_=W["sb3p"][:, :]), dma=True)
            w2_dma[e8] = S.emit("sync", lambda e, bp=bp: e.dma_start(
                out=w2_s[bp][:].rearrange("p (cf d) -> p cf d", cf=8),
                in_=W["sw2T"].rearrange("(cf p) d -> p cf d", p=128)),
                waits=[x for x in [y_ev.get(e8 - 2)] if x], dma=True)
        else:
            b_dma[e8] = S.emit("sync", lambda e, e8=e8, bp=bp: e.dma_start(
                out=b1_s[bp][:], in_=W["b1p"].rearrange("n p f -> p n f")[:, e8, :]),
                waits=[x for x in wdep if x], dma=True)
            b_dma[(e8, 3)] = S.emit("sync", lambda e, e8=e8, bp=bp: e.dma_start(
                out=b3_s[bp][:], in_=W["b3p"].rearrange("n p f -> p n f")[:, e8, :]), dma=True)
            w2_dma[e8] = S.emit("sync", lambda e, e8=e8, bp=bp: e.dma_start(
                out=w2_s[bp][:].rearrange("p (cf d) -> p cf d", cf=8),
                in_=W["w2T"].rearrange("n (cf p) d -> p n cf d", p=128)[:, e8, :, :]),
                waits=[x for x in [y_ev.get(e8 - 2)] if x], dma=True)
        last_a = None
        prev_fi_mm = {}
        for Fi in range(8):
            fp = Fi % 2
            fidep = [prev_fi_mm.get(Fi - 2), a_ms.get(e8 - 1) if Fi == 0 else None,
                     a_ms.get(e8 - 2) if Fi == 1 else None]
            for q_ in range(2):
                for wo_ in range(2):
                    osl = w13_s[fp][:, q_ * 2048 + wo_ * 128::256].rearrange("p x -> p x")                         if False else None
                    out_ap = w13_s[fp][:, q_ * 2048:(q_ + 1) * 2048].rearrange(
                        "p (cd wo fl) -> p cd wo fl", cd=8, wo=2)[:, :, wo_, :]
                    if is_sh:
                        w13_dma[(e8, Fi)] = S.emit("sync", lambda e, fp=fp, Fi=Fi, q_=q_, wo_=wo_,
                                                   out_ap=out_ap: e.dma_start(
                            out=out_ap,
                            in_=W["sw13T"].rearrange("q (cd p) (wo fi fl) -> p q cd wo fi fl",
                                                     p=128, wo=2, fl=128)[:, q_, :, wo_, Fi, :]),
                            waits=[x for x in fidep if x] if (q_ == 0 and wo_ == 0) else [], dma=True)
                    else:
                        w13_dma[(e8, Fi)] = S.emit("sync", lambda e, e8=e8, fp=fp, Fi=Fi, q_=q_, wo_=wo_,
                                                   out_ap=out_ap: e.dma_start(
                            out=out_ap,
                            in_=W["w13T"].rearrange("q n (cd p) (wo fi fl) -> p q n cd wo fi fl",
                                                    p=128, wo=2, fl=128)[:, q_, e8, :, wo_, Fi, :]),
                            waits=[x for x in fidep if x] if (q_ == 0 and wo_ == 0) else [], dma=True)
            for tb in range(4):
                par = tb % 2
                gb_ = (0, 1) if par == 0 else (2, 3)
                lb_ = (4, 5) if par == 0 else (6, 7)
                scrp = (Fi * 4 + tb) % 2
                extra = [w13_dma[(e8, Fi)], b_dma[e8], b_dma[(e8, 3)], tb_ready[tb]]
                mm_done = {}
                for qi, (bidx, wq, woff) in enumerate((
                        (gb_[0], 0, 0), (gb_[1], 1, 0),
                        (lb_[0], 0, 1), (lb_[1], 1, 1))):
                    for cdx in range(DCH):
                        m = S.emit("tensor", lambda e, bidx=bidx, wq=wq, woff=woff, Fi=Fi,
                                   tb=tb, cdx=cdx, fp=fp: e.matmul(
                            psum[bidx][:, :],
                            w13_s[fp][:, wq * 2048 + cdx * 256 + woff * 128:
                                       wq * 2048 + cdx * 256 + woff * 128 + 128],
                            xln_fm[:, cdx * 2048 + tb * 512: cdx * 2048 + (tb + 1) * 512],
                            start=(cdx == 0), stop=(cdx == DCH - 1)),
                            waits=mm_waits(bidx, extra) if cdx == 0 else [],
                            inc=(1 if cdx == DCH - 1 else 0))
                    mm_done[qi] = m
                spw = [prev_scr.get(scrp)]
                me1 = S.emit("scalar", lambda e, scrp=scrp, Fi=Fi, bp=bp, b0=gb_[0]: e.activation(
                    scr_e1[scrp][:], psum[b0][:, :], AF.Identity, bias=b1_s[bp][:, Fi:Fi + 1]),
                    waits=[mm_done[0]] + [x for x in spw if x], inc=1)
                mhg = S.emit("vector", lambda e, scrp=scrp, Fi=Fi, bp=bp, b1=gb_[1]: e.scalar_tensor_tensor(
                    scr_hg[scrp][:], psum[b1][:, :], b3_s[bp][:, Fi:Fi + 1], scr_e1[scrp][:],
                    OP.add, OP.mult), waits=[mm_done[1], me1] + [x for x in spw if x], inc=1)
                mgm = S.emit("vector", lambda e, scrp=scrp: e.tensor_scalar_min(
                    scr_hg[scrp][:], scr_hg[scrp][:], 7.0), inc=1)
                msgl = S.emit("scalar", lambda e, scrp=scrp: e.activation(
                    scr_sg[scrp][:], scr_hg[scrp][:], AF.Silu, scale=1.702),
                    waits=[mgm], inc=1)
                ml1 = S.emit("scalar", lambda e, scrp=scrp, Fi=Fi, bp=bp, b2_=lb_[0]: e.activation(
                    scr_l1[scrp][:], psum[b2_][:, :], AF.Identity, bias=b1_s[bp][:, Fi + 8:Fi + 9]),
                    waits=[mm_done[2]] + [x for x in spw if x], inc=1)
                mhl = S.emit("vector", lambda e, scrp=scrp, Fi=Fi, bp=bp, b3_=lb_[1]: e.scalar_tensor_tensor(
                    scr_hl[scrp][:], psum[b3_][:, :], b3_s[bp][:, Fi + 8:Fi + 9], scr_l1[scrp][:],
                    OP.add, OP.mult), waits=[mm_done[3], ml1], inc=1)
                mlc = S.emit("vector", lambda e, scrp=scrp: e.tensor_scalar(
                    scr_hl[scrp][:], scr_hl[scrp][:], 7.0, -7.0, OP.min, OP.max), inc=1)
                ma = S.emit("vector", lambda e, scrp=scrp, Fi=Fi, tb=tb: e.scalar_tensor_tensor(
                    a_fm[:, Fi * 2048 + tb * 512: Fi * 2048 + (tb + 1) * 512],
                    scr_hl[scrp][:], 1.0, scr_sg[scrp][:], OP.add, OP.mult),
                    waits=[msgl], inc=1)
                prev_scr[scrp] = ma
                prev_fi_mm[Fi] = mm_done[3]
                bank_ev[gb_[0]] = mhg
                bank_ev[gb_[1]] = mhg
                bank_ev[lb_[0]] = mhl
                bank_ev[lb_[1]] = mhl
                last_a = ma
        a_ms[e8] = last_a
        for i in range(TCH):
            for dh in range(2):
                bidx = (0 if i % 2 == 0 else 2) + dh
                extra = [last_a, w2_dma[e8]]
                for cf in range(DCH):
                    m = S.emit("tensor", lambda e, bidx=bidx, i=i, dh=dh, cf=cf, bp=bp: e.matmul(
                        psum[bidx][:, :], a_fm[:, cf * 2048 + i * 128: cf * 2048 + (i + 1) * 128],
                        w2_s[bp][:, cf * 1024 + dh * 512: cf * 1024 + (dh + 1) * 512],
                        start=(cf == 0), stop=(cf == DCH - 1)),
                        waits=mm_waits(bidx, extra) if cf == 0 else [],
                        inc=(1 if cf == DCH - 1 else 0))
                if is_sh:
                    mld = S.emit("sync", lambda e, i=i: e.dma_start(out=xch[i % 2][:], in_=x_t[i]),
                                 waits=[x for x in [fin[i - 2] if i >= 2 else None,
                                                    cons_m5[15][0], cons_m5[15][1]] if x], dma=True) \
                        if dh == 0 else mld
                    mt1 = S.emit("vector", lambda e, bidx=bidx, i=i, dh=dh: e.scalar_tensor_tensor(
                        xsq[:, dh * 512:(dh + 1) * 512], psum[bidx][:, :], 1.0 / 1.702,
                        sb2_b[:, dh * 512:(dh + 1) * 512], OP.mult, OP.add),
                        waits=[m, m_sb2b, m_sb2b2], inc=1)
                    mt2 = S.emit("vector", lambda e, i=i, dh=dh: e.tensor_tensor(
                        xsq[:, dh * 512:(dh + 1) * 512], xsq[:, dh * 512:(dh + 1) * 512],
                        delta[:, i * 1024 + dh * 512: i * 1024 + (dh + 1) * 512], OP.add),
                        waits=[delta_ms[(7, i, dh)]], inc=1)
                    mt3 = S.emit("vector", lambda e, i=i, dh=dh: e.tensor_tensor(
                        xch[i % 2][:, dh * 512:(dh + 1) * 512], xch[i % 2][:, dh * 512:(dh + 1) * 512],
                        xsq[:, dh * 512:(dh + 1) * 512], OP.add),
                        waits=[mld], inc=1)
                    bank_ev[bidx] = mt3
                    if dh == 1:
                        mst = S.emit("gpsimd", lambda e, i=i: e.dma_start(
                            out=xo_t[i], in_=xch[i % 2][:]),
                            waits=[mt3], dma=True)
                        fin.append(mst)
                else:
                    if e8 == 0:
                        md = S.emit("vector", lambda e, bidx=bidx, i=i, dh=dh, e8=e8: e.tensor_scalar_mul(
                            delta[:, i * 1024 + dh * 512: i * 1024 + (dh + 1) * 512],
                            psum[bidx][:, :], cw[:, i * 8 + e8: i * 8 + e8 + 1]),
                            waits=[m, cw_ms[i]], inc=1)
                    else:
                        md = S.emit("vector", lambda e, bidx=bidx, i=i, dh=dh, e8=e8: e.scalar_tensor_tensor(
                            delta[:, i * 1024 + dh * 512: i * 1024 + (dh + 1) * 512],
                            psum[bidx][:, :], cw[:, i * 8 + e8: i * 8 + e8 + 1],
                            delta[:, i * 1024 + dh * 512: i * 1024 + (dh + 1) * 512],
                            OP.mult, OP.add),
                            waits=[m, delta_ms[(e8 - 1, i, dh)]], inc=1)
                    delta_ms[(e8, i, dh)] = md
                    bank_ev[bidx] = md
        y_ev[e8] = md if not is_sh else y_ev.get(7)
    dbg = dict(xln_fm=xln_fm, a_fm=a_fm, delta=delta, cw=cw, glog=glog, ind1=ind1, gstat=gstat)
    return fin, dbg
import numpy as np
from contextlib import ExitStack
from concourse.bass_utils import run_bass_kernel_spmd

_W0A = (("wq0", [1024, 1024], BF16), ("wk0", [1024, 1024], BF16),
        ("wv0", [1024, 2048], BF16), ("wg0", [1024, 2048], BF16),
        ("wo0", [2048, 1024], BF16), ("gnw0", [1, 256], F32),
        ("ropeq", [2048, 2048], BF16), ("ropek", [2048, 2048], BF16))
_W0F = (("phiwT", [1024, 2048], BF16), ("phib", [1, 2048], BF16),
        ("expw", [8, 1024, 1024], BF16), ("expb", [1, 8192], F32))
_W1A = (("qw1T", [1024, 1024], BF16), ("kw1T", [1024, 1024], BF16),
        ("vw1T", [1024, 1024], BF16), ("ow1T", [1024, 1024], BF16),
        ("csw1T", [1024, 256], BF16), ("csb1", [128, 2], F32),
        ("csw2T", [256, 16], BF16), ("csb2", [1, 16], F32),
        ("dparams", [1, 1024], F32))
_W1F = (("gatewT", [1024, 8], BF16), ("gateb", [1, 8], F32),
        ("w13T", [2, 8, 1024, 2048], BF16), ("b1p", [8, 128, 16], F32),
        ("b3p", [8, 128, 16], F32), ("w2T", [8, 1024, 1024], BF16),
        ("b2", [8, 1024], F32), ("sw13T", [2, 1024, 2048], BF16),
        ("sb1p", [128, 16], F32), ("sb3p", [128, 16], F32),
        ("sw2T", [1024, 1024], BF16), ("sb2", [1, 1024], F32))
_WC = (("maskst", [128, 2048], BF16), ("lotri32", [128, 128], F32),
       ("ustri32", [128, 128], F32), ("maskbd32", [128, 128], BF16),
       ("sel128", [128, 32], BF16), ("rep32", [32, 128], F32))
_ALLW = tuple(list(_W0A) + list(_W0F) + list(_W1A) + list(_W1F) + list(_WC))


def _build_fused():
    import concourse.bass as bass
    nc = bass.Bass("TRN2", target_bir_lowering=False)
    x_in = nc.dram_tensor("x", [2048, 1024], F32, kind="ExternalInput").ap()
    x_out = nc.dram_tensor("xout", [2048, 1024], F32, kind="ExternalOutput").ap()
    x1 = nc.dram_tensor("xr1", [2048, 1024], F32).ap()
    x2 = nc.dram_tensor("xr2", [2048, 1024], F32).ap()
    x3 = nc.dram_tensor("xr3", [2048, 1024], F32).ap()
    W = {}
    for nm, shp, dt in _ALLW:
        W[nm] = nc.dram_tensor(nm, shp, dt, kind="ExternalInput").ap()
    es = ExitStack()
    with es:
        S = Sched(nc, es)
        psum = []
        for i in range(8):
            psum.append(es.enter_context(nc.psum_tensor(f"pb{i}", [128, 512], F32))[:])
        c = alloc_consts(nc, es, S)
        cm = emit_consts(nc, S, c, W)
        with ExitStack() as es0:
            emit_l0_attn(nc, S, es0, c, psum, x_in, x1, W, cm)
            S.barrier()
        with ExitStack() as es1:
            emit_l0_ffn(nc, S, es1, c, psum, x1, x2, W, cm, 0)
            S.barrier()
        with ExitStack() as es2:
            emit_l1_attn(nc, S, es2, c, psum, x2, x3, W, cm)
            S.barrier()
        with ExitStack() as es3:
            emit_l1_ffn(nc, S, es3, c, psum, x3, x_out, W, cm)
        S.finish()
        S.build()
    return nc


_cache = {}


def kernel(**inputs):
    Wh = host_prep(inputs)
    percore = [host_prep_percore(inputs, core) for core in range(8)]
    if "nc" not in _cache:
        _cache["nc"] = _build_fused()
    nc = _cache["nc"]
    in_maps = []
    for core in range(8):
        m = {"x": percore[core]["x_in"]}
        for nm, shp, dt in _ALLW:
            m[nm] = percore[core][nm] if nm in percore[core] else Wh[nm]
        in_maps.append(m)
    res = run_bass_kernel_spmd(nc, in_maps, core_ids=list(range(8)))
    return np.stack([np.asarray(res.results[core]["xout"], dtype=np.float32)
                     for core in range(8)], axis=0)
